# version 10
# baseline (speedup 1.0000x reference)
import contextlib
import numpy as np
import ml_dtypes
import concourse.bass as bass
import concourse.mybir as mybir
from concourse.bass_utils import run_bass_kernel_spmd

F32 = mybir.dt.float32
BF16 = mybir.dt.bfloat16
AF = mybir.ActivationFunctionType
ALU = mybir.AluOpType
AX = mybir.AxisListType

D = 4096
S = 2048
TOWN = 1024
NKC = 32
ISQD = 0.08838834764831845
LN_EPS = 1e-5

def XB(kc):  return kc * 2048
def PT(b):   return b * 8192
def QH(b):   return 16384 + b * 1024
def KH(b):   return 18432 + b * 2048
def VH(b):   return 22528 + b * 2048
def CST(b):  return 26624 + b * 1024
def CTX(h):  return h * 1024
def ATB(kc): return 32768 + kc * 1024
def X1B(kc): return kc * 1024
AT0 = 32768
BT0 = 36864


def build_program(dump=False):
    nc = bass.Bass("TRN2", target_bir_lowering=False, debug=False)
    skind = {"kind": "ExternalOutput"} if dump else {}

    xT = nc.dram_tensor("xT", [D, S], F32, kind="ExternalInput").ap()
    wqT = nc.dram_tensor("wqT", [D, D], F32, kind="ExternalInput").ap()
    wkT = nc.dram_tensor("wkT", [D, D], F32, kind="ExternalInput").ap()
    wvT = nc.dram_tensor("wvT", [D, D], F32, kind="ExternalInput").ap()
    woT = nc.dram_tensor("woT", [D, D], F32, kind="ExternalInput").ap()
    wbT = nc.dram_tensor("wbT", [D, D], F32, kind="ExternalInput").ap()
    bbC = nc.dram_tensor("bbC", [128, 32], F32, kind="ExternalInput").ap()
    wrT = nc.dram_tensor("wrT", [D, 8], F32, kind="ExternalInput").ap()
    ATd = nc.dram_tensor("ATd", [D, 128], F32, kind="ExternalInput").ap()
    BTd = nc.dram_tensor("BTd", [128, D], F32, kind="ExternalInput").ap()
    identI = nc.dram_tensor("identI", [128, 128], BF16, kind="ExternalInput").ap()
    identF = nc.dram_tensor("identF", [128, 128], F32, kind="ExternalInput").ap()
    REPI = nc.dram_tensor("REPI", [8, 128], BF16, kind="ExternalInput").ap()
    yT = nc.dram_tensor("yT", [D, TOWN], F32, kind="ExternalOutput").ap()

    q_dram = nc.dram_tensor("q_dram", [32, 128, 1024], BF16, **skind).ap()
    k_dram = nc.dram_tensor("k_dram", [32, 128, 2048], BF16, **skind).ap()
    v_dram = nc.dram_tensor("v_dram", [32, 128, 2048], BF16, **skind).ap()
    ctx_dram = nc.dram_tensor("ctx_dram", [32, 128, 1024], BF16, **skind).ap()
    x1_dram = nc.dram_tensor("x1_dram", [32, 128, 1024], F32, **skind).ap()
    ffn_dram = nc.dram_tensor("ffn_dram", [32, 128, 1024], BF16, **skind).ap()
    if dump:
        xn_dram = nc.dram_tensor("xn_dram", [32, 128, 2048], BF16,
                                 kind="ExternalOutput").ap()
        xn2_dram = nc.dram_tensor("xn2_dram", [16, 128, 2048], BF16,
                                  kind="ExternalOutput").ap()

    cnt = {}

    def bump(name, n=1):
        cnt[name] = cnt.get(name, 0) + n
        return cnt[name]

    def cur(name):
        return cnt.get(name, 0)

    with contextlib.ExitStack() as stk:
        sbt = lambda n, shp, dt: stk.enter_context(nc.sbuf_tensor(n, shp, dt))
        pst = lambda n, shp: stk.enter_context(nc.psum_tensor(n, shp, F32))

        A = sbt("A", [128, 65536], BF16)
        wbf = [sbt(f"wbf{i}", [128, 4096], BF16) for i in range(2)]
        Df = sbt("Df", [128, 4096], F32)
        s_sq = sbt("s_sq", [128, 4096], BF16)
        s_qkstg = sbt("s_qkstg", [128, 2048], BF16)
        s_ev = sbt("s_ev", [128, 4096], BF16)
        s_hw = sbt("s_hw", [128, 1024], BF16)
        s_rows = sbt("s_rows", [1, 1536], F32)
        s_lg = sbt("s_lg", [8, 512], F32)
        s_wT = sbt("s_wT", [8, 1024], BF16)
        s_rc = sbt("s_rc", [1, 1024], F32)
        s_ident = sbt("s_ident", [128, 128], BF16)
        s_identf = sbt("s_identf", [128, 128], F32)
        s_rep = sbt("s_rep", [8, 128], BF16)
        s_onesc = sbt("s_onesc", [128, 1], BF16)
        s_onesr = sbt("s_onesr", [1, 128], F32)
        s_m1 = sbt("s_m1", [128, 8], F32)
        s_m2 = sbt("s_m2", [128, 8], F32)
        s_e2 = sbt("s_e2", [128, 8], F32)
        s_den = sbt("s_den", [128, 8], F32)
        s_w1 = sbt("s_w1", [128, 8], F32)
        s_w2 = sbt("s_w2", [128, 8], F32)
        s_msk1 = sbt("s_msk1", [128, 64], F32)
        s_msk2 = sbt("s_msk2", [128, 64], F32)
        s_lg2 = sbt("s_lg2", [128, 64], F32)
        s_wf = sbt("s_wf", [128, 64], F32)
        s_rt = sbt("s_rt", [128, 256], BF16)
        s_bb = sbt("s_bb", [128, 32], F32)

        pb = [pst(f"pb{i}", [128, 512]) for i in range(8)]

        sems = {n: stk.enter_context(nc.semaphore(n))
                for n in ("dmx", "dmw0", "dmw1", "dmq0", "dmq1", "dmc",
                          "dmo", "pe", "sce", "vee")}

        blk = stk.enter_context(nc.Block())

        def t_mm(t, out, lhsT, rhs, start, stop):
            t.matmul(out, lhsT, rhs, start=start, stop=stop).then_inc(sems["pe"], 1)
            return bump("pe")

        def t_tr(t, out, in_, ident):
            t.transpose(out, in_, ident).then_inc(sems["pe"], 1)
            return bump("pe")

        def a_act(e, out, in_, func, bias, scale):
            e.activation(out=out, in_=in_, func=func, bias=bias,
                         scale=scale).then_inc(sems["sce"], 1)
            return bump("sce")

        def v_tt(v, out, in0, in1, op):
            v.tensor_tensor(out=out, in0=in0, in1=in1, op=op).then_inc(sems["vee"], 1)
            return bump("vee")

        def v_cp(v, out, in_):
            v.tensor_copy(out=out, in_=in_).then_inc(sems["vee"], 1)
            return bump("vee")

        def vself(v):
            v.wait_ge(sems["vee"], cur("vee"))

        def d_dma(e, sem, out, in_):
            e.dma_start(out=out, in_=in_).then_inc(sems[sem], 16)
            return bump(sem, 16)

        def waits(eng, g):
            for s, m in g.items():
                eng.wait_ge(sems[s], m)

        bankg = [dict() for _ in range(8)]

        def bank_read(i, sem, mark):
            if bankg[i].get(sem, -1) < mark:
                bankg[i][sem] = mark

        def bank_take(eng, i):
            waits(eng, bankg[i])
            bankg[i].clear()

        evg = [dict() for _ in range(8)]
        sqg = [dict() for _ in range(4)]

        wld = []
        wcons = []

        def w_load(src_col_block):
            i = len(wld)

            @blk.gpsimd
            def _(e, i=i, src=src_col_block):
                if i >= 2:
                    e.wait_ge(sems["pe"], wcons[i - 2])
                else:
                    e.wait_ge(sems["vee"], sqv[30 + i])
                dst = wbf[i % 2][:].rearrange("p (kc e) -> p kc e", e=128)
                wld.append(d_dma(e, f"dmw{i % 2}", dst,
                                 src.rearrange("(kc p) e -> p kc e", p=128)))
            return i

        def ln_rows(sumb, sqb, rb, cb, pe_ready):
            mu = s_rows[0:1, 0:512]
            ex2 = s_rows[0:1, 512:1024]
            var = s_rows[0:1, 1024:1536]

            @blk.scalar
            def _(e):
                e.wait_ge(sems["pe"], pe_ready)
                a_act(e, mu, pb[sumb][0:1, :], AF.Copy, 0.0, 1.0 / D)
                a_act(e, ex2, pb[sqb][0:1, :], AF.Copy, 0.0, 1.0 / D)
            m1 = cur("sce")
            bank_read(sumb, "sce", m1)
            bank_read(sqb, "sce", m1)

            @blk.vector
            def _(v):
                v.wait_ge(sems["sce"], m1)
                v_tt(v, var, mu, mu, ALU.mult)
                vself(v)
                v.scalar_tensor_tensor(
                    out=var, in0=ex2, scalar=LN_EPS, in1=var, op0=ALU.add,
                    op1=ALU.subtract).then_inc(sems["vee"], 1)
                bump("vee")
            m2 = cur("vee")

            @blk.scalar
            def _(e):
                e.wait_ge(sems["vee"], m2)
                a_act(e, ex2, var, AF.Sqrt, 0.0, 1.0)
            m3 = cur("sce")

            @blk.vector
            def _(v):
                v.wait_ge(sems["sce"], m3)
                v.reciprocal(out=s_rc[0:1, 0:512], in_=ex2).then_inc(
                    sems["vee"], 1)
                bump("vee")
                vself(v)
                v_tt(v, s_rc[0:1, 512:1024], mu, s_rc[0:1, 0:512], ALU.mult)
            m4 = cur("vee")

            @blk.tensor
            def _(t):
                t.wait_ge(sems["vee"], m4)
                bank_take(t, rb)
                t_mm(t, pb[rb][:, :], s_onesr[0:1, :], s_rc[0:1, 0:512], True, True)
                bank_take(t, cb)
                t_mm(t, pb[cb][:, :], s_onesr[0:1, :], s_rc[0:1, 512:1024],
                     True, True)
            return cur("pe")

        @blk.sync
        def _(e):
            d_dma(e, "dmx", s_ident[:], identI[:, :])
            d_dma(e, "dmx", s_identf[:], identF[:, :])
            d_dma(e, "dmx", s_rep[:], REPI[:, :])
        init_dmx = cur("dmx")

        @blk.gpsimd
        def _(e):
            d_dma(e, "dmc", s_rt[:].rearrange("p (kc e) -> p kc e", e=8),
                  wrT.rearrange("(kc p) e -> p kc e", p=128))
            d_dma(e, "dmc", s_bb[:], bbC[:, :])
            e.wait_ge(sems["dmc"], cur("dmc"))
        init_dmw = cur("dmc")

        @blk.vector
        def _(v):
            v.memset(s_onesc[:], 1.0).then_inc(sems["vee"], 1)
            bump("vee")
            v.memset(s_onesr[:], 1.0).then_inc(sems["vee"], 1)
            bump("vee")
        init_vee = cur("vee")

        xld, sqv, statmm = [], [], []
        for kc in range(NKC):
            @blk.gpsimd
            def _(e, kc=kc):
                if kc >= 2:
                    e.wait_ge(sems["vee"], sqv[kc - 2])
                xld.append(d_dma(e, f"dmw{kc % 2}", A[:, XB(kc):XB(kc) + 2048],
                                 xT[kc * 128:(kc + 1) * 128, :]))

            @blk.vector
            def _(v, kc=kc):
                v.wait_ge(sems[f"dmw{kc % 2}"], xld[kc])
                if kc >= 2:
                    v.wait_ge(sems["pe"], statmm[kc - 2])
                sl = (kc % 2) * 2048
                sqv.append(v_tt(v, s_sq[:, sl:sl + 2048],
                                A[:, XB(kc):XB(kc) + 2048],
                                A[:, XB(kc):XB(kc) + 2048], ALU.mult))

            @blk.tensor
            def _(t, kc=kc):
                if kc == 0:
                    t.wait_ge(sems["vee"], init_vee)
                t.wait_ge(sems["vee"], sqv[kc])
                st, sp = kc == 0, kc == NKC - 1
                sl = (kc % 2) * 2048
                for tt in range(4):
                    t_mm(t, pb[tt][0:1, :], s_onesc[:],
                         A[:, XB(kc) + tt * 512:XB(kc) + (tt + 1) * 512], st, sp)
                for tt in range(4):
                    m = t_mm(t, pb[4 + tt][0:1, :], s_onesc[:],
                             s_sq[:, sl + tt * 512:sl + (tt + 1) * 512], st, sp)
                statmm.append(m)

        ln1_rc = []
        for tt in range(4):
            ln1_rc.append(ln_rows(tt, 4 + tt, tt, 4 + tt,
                                  statmm[-1] if tt == 0 else ln1_rc[-1]))

        p2v = 0
        for kc in range(NKC):
            @blk.vector
            def _(v, kc=kc):
                if kc == 0:
                    v.wait_ge(sems["pe"], ln1_rc[-1])
                for tt in range(4):
                    tp = Df[:, (tt % 2) * 512:(tt % 2) * 512 + 512]
                    xs = A[:, XB(kc) + tt * 512:XB(kc) + (tt + 1) * 512]
                    if tt >= 2:
                        vself(v)
                    v_tt(v, tp, xs, pb[tt][:, :], ALU.mult)
                    vself(v)
                    v_tt(v, xs, tp, pb[4 + tt][:, :], ALU.subtract)
        p2v = cur("vee")
        for tt in range(4):
            bank_read(tt, "vee", p2v)
            bank_read(4 + tt, "vee", p2v)

        if dump:
            @blk.sync
            def _(e):
                for kc in range(NKC):
                    e.wait_ge(sems["vee"], p2v)
                    d_dma(e, "dmx", xn_dram[kc, :, :],
                          A[:, XB(kc):XB(kc) + 2048])

        ev_i = 0

        def evac_store(pe_mark, bank, dst):
            nonlocal ev_i
            sl = ev_i % 8
            ev_i += 1

            @blk.vector
            def _(v, sl=sl, bank=bank, pe_mark=pe_mark):
                v.wait_ge(sems["pe"], pe_mark)
                if evg[sl]:
                    v.wait_ge(sems["dmx"], cur("dmx"))
                    evg[sl].clear()
                v_cp(v, s_ev[:, sl * 512:(sl + 1) * 512], pb[bank][:, :])
            m = cur("vee")
            bank_read(bank, "vee", m)

            @blk.sync
            def _(e, sl=sl, dst=dst, m=m):
                e.wait_ge(sems["vee"], m)
                d_dma(e, "dmx", dst, s_ev[:, sl * 512:(sl + 1) * 512])
            evg[sl]["dmx"] = cur("dmx")
            return cur("dmx")

        def qk_matrix(wsrc, dram, ntt, first_extra_vee=None):
            for eb in range(32):
                i = w_load(wsrc[:, eb * 128:(eb + 1) * 128])

                @blk.tensor
                def _(t, i=i, eb=eb, ntt=ntt,
                      fx=(first_extra_vee if eb == 0 else None)):
                    if fx is not None:
                        t.wait_ge(sems["vee"], fx)
                    t.wait_ge(sems[f"dmw{i % 2}"], wld[i])
                    for tt in range(ntt):
                        bank_take(t, tt)
                    for kc in range(NKC):
                        for tt in range(ntt):
                            m = t_mm(t, pb[tt][:, :],
                                     wbf[i % 2][:, kc * 128:(kc + 1) * 128],
                                     A[:, XB(kc) + tt * 512:XB(kc) + (tt + 1) * 512],
                                     kc == 0, kc == NKC - 1)
                    _qk_last[0] = m
                last = _qk_last[0]
                for tt in range(ntt):
                    evac_store(last, tt, dram[eb, :, tt * 512:(tt + 1) * 512])
                wcons.append(last)

        _qk_last = [0]
        qk_matrix(wqT, q_dram, 2, first_extra_vee=p2v)
        qk_matrix(wkT, k_dram, 4)

        for eb in range(32):
            i = w_load(wvT[:, eb * 128:(eb + 1) * 128])

            @blk.tensor
            def _(t, i=i):
                t.wait_ge(sems[f"dmw{i % 2}"], wld[i])
                for tt in range(4):
                    bank_take(t, tt)
                for kc in range(NKC):
                    for tt in range(4):
                        m = t_mm(t, pb[tt][:, :],
                                 wbf[i % 2][:, kc * 128:(kc + 1) * 128],
                                 A[:, XB(kc) + tt * 512:XB(kc) + (tt + 1) * 512],
                                 kc == 0, kc == NKC - 1)
                _qk_last[0] = m
            mlast = _qk_last[0]
            wcons.append(mlast)
            stgv = []
            for tt in range(4):
                @blk.vector
                def _(v, tt=tt, mlast=mlast, eb=eb):
                    v.wait_ge(sems["pe"], mlast)
                    if eb > 0 and tt == 0:
                        v.wait_ge(sems["pe"], _vtr_last[0])
                    v_cp(v, Df[:, tt * 512:(tt + 1) * 512], pb[tt][:, :])
                stgv.append(cur("vee"))
                bank_read(tt, "vee", stgv[-1])
            for ktg in range(4):
                tbank = 4 + ktg % 2

                @blk.tensor
                def _(t, ktg=ktg, tbank=tbank):
                    t.wait_ge(sems["vee"], stgv[ktg])
                    bank_take(t, tbank)
                    for j in range(4):
                        kt = ktg * 4 + j
                        m = t_tr(t, pb[tbank][:, j * 128:(j + 1) * 128],
                                 Df[:, kt * 128:(kt + 1) * 128], s_identf[:])
                    _qk_last[0] = m
                _vtr_last[0] = _qk_last[0]
                evac_store(_qk_last[0], tbank,
                           v_dram[eb, :, ktg * 512:(ktg + 1) * 512])
        p3_pe = cur("pe")

        qkvld = {}
        headlast = {}
        expm = {}
        avm = {}
        ctxmul = {}
        drm = {}
        bcm = {}
        ctxdma = {}

        def load_head(e, h):
            b = h % 2
            g = {"pe": headlast[h - 2]} if h - 2 in headlast else {"pe": p3_pe}
            waits(e, g)
            d_dma(e, f"dmq{b}", A[:, QH(b):QH(b) + 1024], q_dram[h, :, :])
            d_dma(e, f"dmq{b}", A[:, KH(b):KH(b) + 2048], k_dram[h, :, :])
            qkvld[h] = d_dma(e, f"dmq{b}", A[:, VH(b):VH(b) + 2048],
                             v_dram[h, :, :])

        p3_dmx = cur("dmx")

        @blk.sync
        def _(e):
            e.wait_ge(sems["dmx"], p3_dmx)
            load_head(e, 0)
            load_head(e, 1)

        for h in range(32):
            hb = h % 2
            for qt in range(2):
                i = h * 2 + qt
                ib = i % 2

                for kt in range(16):
                    bank = kt % 2

                    @blk.tensor
                    def _(t, h=h, qt=qt, i=i, hb=hb, kt=kt, bank=bank):
                        if qt == 0 and kt == 0:
                            t.wait_ge(sems[f"dmq{h % 2}"], qkvld[h])
                        bank_take(t, bank)
                        _scm[(i, kt)] = t_mm(
                            t, pb[bank][:, :],
                            A[:, KH(hb) + kt * 128:KH(hb) + (kt + 1) * 128],
                            A[:, QH(hb) + qt * 512:QH(hb) + (qt + 1) * 512],
                            True, True)

                    @blk.scalar
                    def _(e, i=i, ib=ib, kt=kt, bank=bank):
                        e.wait_ge(sems["pe"], _scm[(i, kt)])
                        if (i - 2, kt) in avm:
                            e.wait_ge(sems["pe"], avm[(i - 2, kt)])
                        elif i < 2:
                            e.wait_ge(sems["pe"], p3_pe)
                        m = a_act(e, A[:, PT(ib) + kt * 512:PT(ib) + (kt + 1) * 512],
                                  pb[bank][:, :], AF.Exp, 0.0, ISQD)
                        expm[(i, kt)] = m
                        bank_read(bank, "sce", m)

                cb = 2 + ib
                db = 4 + ib

                @blk.tensor
                def _(t, i=i, ib=ib, hb=hb, cb=cb, db=db):
                    bank_take(t, cb)
                    bank_take(t, db)
                    for kt in range(16):
                        t.wait_ge(sems["sce"], expm[(i, kt)])
                        pt = A[:, PT(ib) + kt * 512:PT(ib) + (kt + 1) * 512]
                        m = t_mm(t, pb[cb][:, :],
                                 A[:, VH(hb) + kt * 128:VH(hb) + (kt + 1) * 128],
                                 pt, kt == 0, kt == 15)
                        avm[(i, kt)] = m
                        t_mm(t, pb[db][0:1, :], s_onesc[:], pt, kt == 0, kt == 15)
                    _scm[("den", i)] = cur("pe")

                @blk.vector
                def _(v, i=i, db=db):
                    v.wait_ge(sems["pe"], _scm[("den", i)])
                    if i - 1 in bcm:
                        v.wait_ge(sems["pe"], bcm[i - 1])
                    v_cp(v, s_rc[0:1, 0:512], pb[db][0:1, :])
                    vself(v)
                    v.reciprocal(out=s_rc[0:1, 512:1024],
                                 in_=s_rc[0:1, 0:512]).then_inc(
                        sems["vee"], 1)
                    drm[i] = bump("vee")
                    bank_read(db, "vee", drm[i])

                Db = 6 + ib

                @blk.tensor
                def _(t, i=i, Db=Db):
                    t.wait_ge(sems["vee"], drm[i])
                    bank_take(t, Db)
                    bcm[i] = t_mm(t, pb[Db][:, :], s_onesr[0:1, :],
                                  s_rc[0:1, 512:1024], True, True)

                @blk.vector
                def _(v, i=i, h=h, qt=qt, hb=hb, cb=cb, Db=Db):
                    v.wait_ge(sems["pe"], bcm[i])
                    if qt == 0 and h - 2 in ctxdma:
                        v.wait_ge(sems["dmx"], cur("dmx"))
                    cs = A[:, CST(hb) + qt * 512:CST(hb) + (qt + 1) * 512]
                    v_cp(v, cs, pb[cb][:, :])
                    vself(v)
                    ctxmul[i] = v_tt(v, cs, cs, pb[Db][:, :], ALU.mult)
                    bank_read(cb, "vee", ctxmul[i])
                    bank_read(Db, "vee", ctxmul[i])

            headlast[h] = cur("pe")

            @blk.sync
            def _(e, h=h, hb=hb):
                if h + 2 < 32:
                    load_head(e, h + 2)
                e.wait_ge(sems["vee"], ctxmul[h * 2 + 1])
                ctxdma[h] = d_dma(e, "dmx", ctx_dram[h, :, :],
                                  A[:, CST(hb):CST(hb) + 1024])

        p4_pe = cur("pe")
        p4_vee = cur("vee")

        if dump:
            @blk.sync
            def _(e):
                e.wait_ge(sems["pe"], p4_pe)
                for kc in range(16, NKC):
                    d_dma(e, "dmx", xn2_dram[kc - 16, :, :],
                          A[:, XB(kc):XB(kc) + 2048])

        @blk.sync
        def _(e):
            e.wait_ge(sems["pe"], p4_pe)
            e.wait_ge(sems["vee"], p4_vee)
            e.wait_ge(sems["dmx"], ctxdma[31])
            d_dma(e, "dmx", A[:, 0:32768].rearrange("p (h t) -> p h t", t=1024),
                  ctx_dram.rearrange("h p t -> p h t"))
        ctxrb = cur("dmx")

        atv = {}
        for ob in range(32):
            i = w_load(woT[:, ob * 128:(ob + 1) * 128])
            b0 = 0 if ob % 2 == 0 else 6

            @blk.tensor
            def _(t, i=i, ob=ob, b0=b0):
                if ob == 0:
                    t.wait_ge(sems["dmx"], ctxrb)
                t.wait_ge(sems[f"dmw{i % 2}"], wld[i])
                bank_take(t, b0)
                bank_take(t, b0 + 1)
                for kc in range(NKC):
                    for qt in range(2):
                        m = t_mm(t, pb[b0 + qt][:, :],
                                 wbf[i % 2][:, kc * 128:(kc + 1) * 128],
                                 A[:, CTX(kc) + qt * 512:CTX(kc) + (qt + 1) * 512],
                                 kc == 0, kc == NKC - 1)
                _qk_last[0] = m
            mlast = _qk_last[0]
            wcons.append(mlast)
            for qt in range(2):
                bank = b0 + qt
                sqs = (ob * 2 + qt) % 4

                @blk.vector
                def _(v, ob=ob, qt=qt, bank=bank, mlast=mlast, sqs=sqs):
                    v.wait_ge(sems["pe"], mlast)
                    waits(v, sqg[sqs])
                    sqg[sqs].clear()
                    dst = A[:, ATB(ob) + qt * 512:ATB(ob) + (qt + 1) * 512]
                    v_cp(v, dst, pb[bank][:, :])
                    vself(v)
                    atv[(ob, qt)] = v_tt(v, s_sq[:, sqs * 1024:sqs * 1024 + 512],
                                         dst, dst, ALU.mult)
                bank_read(bank, "vee", atv[(ob, qt)])

                @blk.tensor
                def _(t, ob=ob, qt=qt, sqs=sqs):
                    t.wait_ge(sems["vee"], atv[(ob, qt)])
                    st, sp = ob == 0, ob == 31
                    dst = A[:, ATB(ob) + qt * 512:ATB(ob) + (qt + 1) * 512]
                    if st:
                        bank_take(t, 2 + qt)
                        bank_take(t, 4 + qt)
                    t_mm(t, pb[2 + qt][0:1, :], s_onesc[:], dst, st, sp)
                    m = t_mm(t, pb[4 + qt][0:1, :], s_onesc[:],
                             s_sq[:, sqs * 1024:sqs * 1024 + 512], st, sp)
                    sqg[sqs]["pe"] = m

        ln2_rc = []
        for qt in range(2):
            ln2_rc.append(ln_rows(2 + qt, 4 + qt, 2 + qt, 4 + qt,
                                  cur("pe") if qt == 0 else ln2_rc[-1]))

        xol, x1wr, p6v = {}, {}, {}
        for kc in range(NKC):
            @blk.sync
            def _(e, kc=kc):
                if kc >= 2:
                    e.wait_ge(sems["vee"], p6v[kc - 2])
                xol[kc] = d_dma(e, "dmx",
                                Df[:, (kc % 2) * 1024:(kc % 2) * 1024 + 1024],
                                xT[kc * 128:(kc + 1) * 128, 0:1024])

            sqs = kc % 4

            @blk.vector
            def _(v, kc=kc, sqs=sqs):
                v.wait_ge(sems["dmx"], cur("dmx"))
                if kc == 0:
                    v.wait_ge(sems["pe"], ln2_rc[-1])
                waits(v, sqg[sqs])
                sqg[sqs].clear()
                for qt in range(2):
                    xt = Df[:, 2048 + (kc % 2) * 1024 + qt * 512:
                            2048 + (kc % 2) * 1024 + (qt + 1) * 512]
                    ats = A[:, ATB(kc) + qt * 512:ATB(kc) + (qt + 1) * 512]
                    v_tt(v, xt, ats, pb[2 + qt][:, :], ALU.mult)
                    vself(v)
                    v_tt(v, xt, xt, pb[4 + qt][:, :], ALU.subtract)
                    vself(v)
                    v_tt(v, xt, xt,
                         Df[:, (kc % 2) * 1024 + qt * 512:
                             (kc % 2) * 1024 + (qt + 1) * 512], ALU.add)
                    vself(v)
                _x1m[0] = cur("vee")
                for qt in range(2):
                    xt = Df[:, 2048 + (kc % 2) * 1024 + qt * 512:
                            2048 + (kc % 2) * 1024 + (qt + 1) * 512]
                    x1s = A[:, X1B(kc) + qt * 512:X1B(kc) + (qt + 1) * 512]
                    v_cp(v, x1s, xt)
                    vself(v)
                    v_tt(v, s_sq[:, sqs * 1024 + qt * 512:
                                 sqs * 1024 + (qt + 1) * 512],
                         x1s, x1s, ALU.mult)
                p6v[kc] = cur("vee")
            m_x1 = _x1m[0]

            @blk.sync
            def _(e, kc=kc, m_x1=m_x1):
                e.wait_ge(sems["vee"], m_x1)
                x1wr[kc] = d_dma(e, "dmx", x1_dram[kc, :, :],
                                 Df[:, 2048 + (kc % 2) * 1024:
                                     2048 + (kc % 2) * 1024 + 1024])

            @blk.tensor
            def _(t, kc=kc, sqs=sqs):
                t.wait_ge(sems["vee"], p6v[kc])
                st, sp = kc == 0, kc == 31
                if st:
                    for qt in range(2):
                        bank_take(t, 0 + qt)
                        bank_take(t, 6 + qt)
                for qt in range(2):
                    x1s = A[:, X1B(kc) + qt * 512:X1B(kc) + (qt + 1) * 512]
                    t_mm(t, pb[0 + qt][0:1, :], s_onesc[:], x1s, st, sp)
                    m = t_mm(t, pb[6 + qt][0:1, :], s_onesc[:],
                             s_sq[:, sqs * 1024 + qt * 512:
                                  sqs * 1024 + (qt + 1) * 512], st, sp)
                    sqg[sqs]["pe"] = m

        ln3_rc = []
        for qt in range(2):
            ln3_rc.append(ln_rows(0 + qt, 6 + qt, 0 + qt, 6 + qt,
                                  cur("pe") if qt == 0 else ln3_rc[-1]))

        p7av = 0
        for kc in range(NKC):
            @blk.vector
            def _(v, kc=kc):
                if kc == 0:
                    v.wait_ge(sems["pe"], ln3_rc[-1])
                    v.wait_ge(sems["dmx"], cur("dmx"))
                for qt in range(2):
                    tp = Df[:, 2048 + qt * 512:2048 + (qt + 1) * 512]
                    x1s = A[:, X1B(kc) + qt * 512:X1B(kc) + (qt + 1) * 512]
                    vself(v)
                    v_tt(v, tp, x1s, pb[0 + qt][:, :], ALU.mult)
                    vself(v)
                    v_tt(v, x1s, tp, pb[6 + qt][:, :], ALU.subtract)
        p7av = cur("vee")
        for qt in range(2):
            bank_read(0 + qt, "vee", p7av)
            bank_read(6 + qt, "vee", p7av)

        @blk.gpsimd
        def _(e):
            e.wait_ge(sems["vee"], p7av)
            d_dma(e, "dmc", A[:, AT0:AT0 + 4096].rearrange(
                "p (kc e) -> p kc e", e=128),
                ATd.rearrange("(kc p) e -> p kc e", p=128))
            d_dma(e, "dmc", A[:, BT0:BT0 + 4096], BTd[:, :])
        abld = cur("dmc")

        lgmm, wexpm, htm, hwv = {}, {}, {}, {}
        for qt in range(2):
            @blk.tensor
            def _(t, qt=qt):
                if qt == 0:
                    t.wait_ge(sems["vee"], p7av)
                    t.wait_ge(sems["dmc"], init_dmw)
                bank_take(t, 2 + qt)
                for kc in range(NKC):
                    m = t_mm(t, pb[2 + qt][0:8, :],
                             s_rt[:, kc * 8:(kc + 1) * 8],
                             A[:, X1B(kc) + qt * 512:X1B(kc) + (qt + 1) * 512],
                             kc == 0, kc == NKC - 1)
                lgmm[qt] = m

            @blk.vector
            def _(v, qt=qt):
                v.wait_ge(sems["pe"], lgmm[qt])
                if qt == 1:
                    v.wait_ge(sems["pe"], _lgtr[3])
                v_cp(v, s_lg[0:8, :], pb[2 + qt][0:8, :])
            lgev = cur("vee")
            bank_read(2 + qt, "vee", lgev)

            @blk.tensor
            def _(t, qt=qt, lgev=lgev):
                t.wait_ge(sems["vee"], lgev)
                if qt == 0:
                    t.wait_ge(sems["dmx"], init_dmx)
                    bank_take(t, 4)
                for c in range(4):
                    _lgtr[c] = t_tr(t, pb[4][:, (qt * 4 + c) * 8:(qt * 4 + c + 1) * 8],
                                    s_lg[0:8, c * 128:(c + 1) * 128],
                                    s_identf[0:8, 0:8])

            @blk.vector
            def _(v, qt=qt):
                for c in range(4):
                    cc = qt * 4 + c
                    v.wait_ge(sems["pe"], _lgtr[c])
                    lgs = pb[4][:, cc * 8:(cc + 1) * 8]
                    v.reduce_max(out=s_m1[:, cc:cc + 1], in_=lgs,
                                 axis=AX.X).then_inc(sems["vee"], 1)
                    bump("vee")
                    vself(v)
                    v.tensor_single_scalar(
                        out=s_msk1[:, cc * 8:(cc + 1) * 8], in_=lgs,
                        scalar=s_m1[:, cc:cc + 1], op=ALU.is_ge).then_inc(
                        sems["vee"], 1)
                    bump("vee")
                    vself(v)
                    v.scalar_tensor_tensor(
                        out=s_lg2[:, cc * 8:(cc + 1) * 8],
                        in0=s_msk1[:, cc * 8:(cc + 1) * 8], scalar=-1e30,
                        in1=lgs, op0=ALU.mult, op1=ALU.add).then_inc(
                        sems["vee"], 1)
                    bump("vee")
                    vself(v)
                    v.reduce_max(out=s_m2[:, cc:cc + 1],
                                 in_=s_lg2[:, cc * 8:(cc + 1) * 8],
                                 axis=AX.X).then_inc(sems["vee"], 1)
                    bump("vee")
                    vself(v)
                    v.tensor_single_scalar(
                        out=s_msk2[:, cc * 8:(cc + 1) * 8],
                        in_=s_lg2[:, cc * 8:(cc + 1) * 8],
                        scalar=s_m2[:, cc:cc + 1], op=ALU.is_ge).then_inc(
                        sems["vee"], 1)
                    bump("vee")
            _tk[qt] = cur("vee")

        @blk.vector
        def _(v):
            v.wait_ge(sems["vee"], _tk[1])
            vself(v)
            v_tt(v, s_den[:, :], s_m2[:, :], s_m1[:, :], ALU.subtract)
        tkd = cur("vee")

        @blk.scalar
        def _(e):
            e.wait_ge(sems["vee"], tkd)
            a_act(e, s_e2[:, :], s_den[:, :], AF.Exp, 0.0, 1.0)
        tke = cur("sce")

        @blk.vector
        def _(v):
            v.wait_ge(sems["sce"], tke)
            v.tensor_scalar(out=s_den[:, :], in0=s_e2[:, :], scalar1=1.0,
                            scalar2=0.0, op0=ALU.add,
                            op1=ALU.add).then_inc(sems["vee"], 1)
            bump("vee")
            vself(v)
            v.reciprocal(out=s_w2[:, :], in_=s_den[:, :]).then_inc(sems["vee"], 1)
            bump("vee")
            vself(v)
            v_tt(v, s_w2[:, :], s_w2[:, :], s_e2[:, :], ALU.mult)
            vself(v)
            v.tensor_scalar(out=s_w1[:, :], in0=s_w2[:, :], scalar1=-1.0,
                            scalar2=1.0, op0=ALU.mult, op1=ALU.add).then_inc(
                sems["vee"], 1)
            bump("vee")
            for cc in range(8):
                vself(v)
                v.tensor_single_scalar(
                    out=s_wf[:, cc * 8:(cc + 1) * 8],
                    in_=s_msk1[:, cc * 8:(cc + 1) * 8],
                    scalar=s_w1[:, cc:cc + 1], op=ALU.mult).then_inc(
                    sems["vee"], 1)
                bump("vee")
                v.tensor_single_scalar(
                    out=s_msk2[:, cc * 8:(cc + 1) * 8],
                    in_=s_msk2[:, cc * 8:(cc + 1) * 8],
                    scalar=s_w2[:, cc:cc + 1], op=ALU.mult).then_inc(
                    sems["vee"], 1)
                bump("vee")
                vself(v)
                v_tt(v, s_wf[:, cc * 8:(cc + 1) * 8],
                     s_wf[:, cc * 8:(cc + 1) * 8],
                     s_msk2[:, cc * 8:(cc + 1) * 8], ALU.add)

        tkw = cur("vee")
        bank_read(4, "vee", tkw)

        @blk.tensor
        def _(t):
            t.wait_ge(sems["vee"], tkw)
            bank_take(t, 5)
            bank_take(t, 6)
            for cc in range(8):
                bk = 5 + cc // 4
                _lgtr[cc % 4] = t_tr(t, pb[bk][0:8, (cc % 4) * 128:
                                               (cc % 4 + 1) * 128],
                                     s_wf[:, cc * 8:(cc + 1) * 8], s_identf[:])
        wtr = cur("pe")

        @blk.vector
        def _(v):
            v.wait_ge(sems["pe"], wtr)
            v_cp(v, s_wT[0:8, 0:512], pb[5][0:8, :])
            v_cp(v, s_wT[0:8, 512:1024], pb[6][0:8, :])
        wtev = cur("vee")
        bank_read(5, "vee", wtev)
        bank_read(6, "vee", wtev)

        for qt in range(2):
            @blk.tensor
            def _(t, qt=qt):
                if qt == 0:
                    t.wait_ge(sems["vee"], wtev)
                bank_take(t, 0 + qt)
                wexpm[qt] = t_mm(t, pb[0 + qt][:, :], s_rep[0:8, :],
                                 s_wT[0:8, qt * 512:(qt + 1) * 512], True, True)
                if qt == 0:
                    t.wait_ge(sems["dmc"], abld)
                bank_take(t, 6 + qt)
                for kc in range(NKC):
                    m = t_mm(t, pb[6 + qt][:, :],
                             A[:, AT0 + kc * 128:AT0 + (kc + 1) * 128],
                             A[:, X1B(kc) + qt * 512:X1B(kc) + (qt + 1) * 512],
                             kc == 0, kc == NKC - 1)
                htm[qt] = m

            @blk.vector
            def _(v, qt=qt):
                v.wait_ge(sems["pe"], htm[qt])
                hs = s_hw[:, qt * 512:(qt + 1) * 512]
                v_cp(v, hs, pb[6 + qt][:, :])
                vself(v)
                hwv[qt] = v_tt(v, hs, hs, pb[0 + qt][:, :], ALU.mult)
            bank_read(0 + qt, "vee", hwv[qt])
            bank_read(6 + qt, "vee", hwv[qt])

        ffv = {}
        for ob in range(32):
            i = w_load(wbT[:, ob * 128:(ob + 1) * 128])
            b0 = 2 if ob % 2 == 0 else 6

            @blk.tensor
            def _(t, i=i, ob=ob, b0=b0):
                if ob == 0:
                    t.wait_ge(sems["vee"], hwv[1])
                t.wait_ge(sems[f"dmw{i % 2}"], wld[i])
                bank_take(t, b0)
                bank_take(t, b0 + 1)
                for kc in range(NKC):
                    for qt in range(2):
                        t_mm(t, pb[b0 + qt][:, :],
                             wbf[i % 2][:, kc * 128:(kc + 1) * 128],
                             A[:, X1B(kc) + qt * 512:X1B(kc) + (qt + 1) * 512],
                             kc == 0, False)
                for qt in range(2):
                    m = t_mm(t, pb[b0 + qt][:, :],
                             A[:, BT0 + ob * 128:BT0 + (ob + 1) * 128],
                             s_hw[:, qt * 512:(qt + 1) * 512], False, True)
                _qk_last[0] = m
            mlast = _qk_last[0]
            wcons.append(mlast)
            for qt in range(2):
                bank = b0 + qt
                sl = (ob * 2 + qt) % 8
                sqs = (ob * 2 + qt) % 4

                @blk.vector
                def _(v, ob=ob, mlast=mlast, bank=bank, sl=sl, sqs=sqs):
                    v.wait_ge(sems["pe"], mlast)
                    if "dmx" in evg[sl]:
                        v.wait_ge(sems["dmx"], cur("dmx"))
                        del evg[sl]["dmx"]
                    waits(v, evg[sl])
                    evg[sl].clear()
                    waits(v, sqg[sqs])
                    sqg[sqs].clear()
                    v.tensor_single_scalar(
                        out=s_ev[:, sl * 512:(sl + 1) * 512],
                        in_=pb[bank][:, :], scalar=s_bb[:, ob:ob + 1],
                        op=ALU.add).then_inc(sems["vee"], 1)
                    bump("vee")
                    vself(v)
                    v_tt(v, s_sq[:, sqs * 1024:sqs * 1024 + 512],
                         s_ev[:, sl * 512:(sl + 1) * 512],
                         s_ev[:, sl * 512:(sl + 1) * 512], ALU.mult)
                ffv[(ob, qt)] = cur("vee")
                bank_read(bank, "vee", ffv[(ob, qt)])

                @blk.sync
                def _(e, ob=ob, qt=qt, sl=sl):
                    e.wait_ge(sems["vee"], ffv[(ob, qt)])
                    d_dma(e, "dmx", ffn_dram[ob, :, qt * 512:(qt + 1) * 512],
                          s_ev[:, sl * 512:(sl + 1) * 512])
                evg[sl]["dmx"] = cur("dmx")

                @blk.tensor
                def _(t, ob=ob, qt=qt, sl=sl, sqs=sqs):
                    t.wait_ge(sems["vee"], ffv[(ob, qt)])
                    st, sp = ob == 0, ob == 31
                    if st:
                        bank_take(t, 4 + qt)
                        bank_take(t, 0 + qt)
                    t_mm(t, pb[4 + qt][0:1, :], s_onesc[:],
                         s_ev[:, sl * 512:(sl + 1) * 512], st, sp)
                    m = t_mm(t, pb[0 + qt][0:1, :], s_onesc[:],
                             s_sq[:, sqs * 1024:sqs * 1024 + 512], st, sp)
                    evg[sl]["pe"] = m
                    sqg[sqs]["pe"] = m

        ln4_rc = []
        for qt in range(2):
            ln4_rc.append(ln_rows(4 + qt, 0 + qt, 4 + qt, 0 + qt,
                                  cur("pe") if qt == 0 else ln4_rc[-1]))

        p8_pre = cur("dmx")
        p8ld, p8v, ydma = {}, {}, {}
        for kc in range(NKC):
            @blk.sync
            def _(e, kc=kc):
                if kc == 0:
                    e.wait_ge(sems["dmx"], p8_pre)
                if kc >= 2:
                    e.wait_ge(sems["vee"], p8v[kc - 2])
                d_dma(e, "dmx", Df[:, (kc % 2) * 1024:(kc % 2) * 1024 + 1024],
                      x1_dram[kc, :, :])
                p8ld[kc] = d_dma(e, "dmx",
                                 s_qkstg[:, (kc % 2) * 1024:(kc % 2) * 1024 + 1024],
                                 ffn_dram[kc, :, :])

            @blk.vector
            def _(v, kc=kc):
                v.wait_ge(sems["dmx"], cur("dmx"))
                if kc == 0:
                    v.wait_ge(sems["pe"], ln4_rc[-1])
                if kc - 2 in ydma:
                    v.wait_ge(sems["dmo"], cur("dmo"))
                for qt in range(2):
                    ys = Df[:, 2048 + (kc % 2) * 1024 + qt * 512:
                            2048 + (kc % 2) * 1024 + (qt + 1) * 512]
                    fs = s_qkstg[:, (kc % 2) * 1024 + qt * 512:
                                 (kc % 2) * 1024 + (qt + 1) * 512]
                    v_tt(v, ys, fs, pb[4 + qt][:, :], ALU.mult)
                    vself(v)
                    v_tt(v, ys, ys, pb[0 + qt][:, :], ALU.subtract)
                    vself(v)
                    v_tt(v, ys, ys,
                         Df[:, (kc % 2) * 1024 + qt * 512:
                             (kc % 2) * 1024 + (qt + 1) * 512], ALU.add)
                p8v[kc] = cur("vee")

            @blk.sync
            def _(e, kc=kc):
                e.wait_ge(sems["vee"], p8v[kc])
                ydma[kc] = d_dma(e, "dmo", yT[kc * 128:(kc + 1) * 128, :],
                                 Df[:, 2048 + (kc % 2) * 1024:
                                     2048 + (kc % 2) * 1024 + 1024])

        @blk.sync
        def _(e):
            e.wait_ge(sems["dmo"], ydma[31])

    return nc


_scm = {}
_lgtr = {}
_tk = {}
_vtr_last = [0]
_x1m = [0]
_x1m_frozen = {}
_p6pe = {}

_NC = None


def _get_nc():
    global _NC
    if _NC is None:
        _scm.clear()
        _lgtr.clear()
        _tk.clear()
        _NC = build_program()
    return _NC


def kernel(**inputs):
    x = np.asarray(inputs["L0_attn_x"], np.float32)
    w_q = np.asarray(inputs["L0_attn_w_q"], np.float32)
    w_k = np.asarray(inputs["L0_attn_w_k"], np.float32)
    w_v = np.asarray(inputs["L0_attn_w_v"], np.float32)
    w_o = np.asarray(inputs["L0_attn_w_o"], np.float32)
    W_base = np.asarray(inputs["L0_ffn_W_base"], np.float32)
    b_base = np.asarray(inputs["L0_ffn_b_base"], np.float32)
    W_router = np.asarray(inputs["L0_ffn_W_router"], np.float32)
    Am = np.asarray(inputs["L0_ffn_A"], np.float32)
    Bm = np.asarray(inputs["L0_ffn_B"], np.float32)

    shared = {
        "wqT": np.ascontiguousarray(w_q.T),
        "wkT": np.ascontiguousarray(w_k.T),
        "wvT": np.ascontiguousarray(w_v.T),
        "woT": np.ascontiguousarray(w_o.T),
        "wbT": np.ascontiguousarray(W_base.T),
        "bbC": np.ascontiguousarray(b_base.reshape(32, 128).T),
        "wrT": np.ascontiguousarray(W_router.T),
        "ATd": np.ascontiguousarray(Am.transpose(2, 0, 1).reshape(D, 128)),
        "BTd": np.ascontiguousarray(Bm.transpose(0, 2, 1).reshape(128, D)),
        "identI": np.eye(128, dtype=ml_dtypes.bfloat16),
        "identF": np.eye(128, dtype=np.float32),
        "REPI": (2.0 * (np.arange(128)[None, :] // 16 ==
                        np.arange(8)[:, None])).astype(ml_dtypes.bfloat16),
    }
    in_maps = []
    for c in range(8):
        b, half = c // 2, c % 2
        own = x[b, half * 1024:(half + 1) * 1024].T
        oth = x[b, (1 - half) * 1024:(2 - half) * 1024].T
        m = dict(shared)
        m["xT"] = np.ascontiguousarray(np.concatenate([own, oth], axis=1))
        in_maps.append(m)

    nc = _get_nc()
    res = run_bass_kernel_spmd(nc, in_maps, core_ids=list(range(8)))
    out = np.empty((4, 2048, D), np.float32)
    for c in range(8):
        b, half = c // 2, c % 2
        out[b, half * 1024:(half + 1) * 1024, :] = res.results[c]["yT"].T
    return out


if __name__ == "__main__":
    nc = _get_nc()
    print("build ok")


# version 11 (final)
# speedup vs baseline: 1.1297x; 1.1297x over previous
import contextlib
import numpy as np
import ml_dtypes
import concourse.bass as bass
import concourse.mybir as mybir
from concourse.bass_utils import run_bass_kernel_spmd

F32 = mybir.dt.float32
BF16 = mybir.dt.bfloat16
AF = mybir.ActivationFunctionType
ALU = mybir.AluOpType
AX = mybir.AxisListType

D = 4096
S = 2048
TOWN = 1024
NKC = 32
ISQD = 0.08838834764831845
LN_EPS = 1e-5

def XB(kc):  return kc * 2048
def PT(b):   return b * 8192
def QH(b):   return 16384 + b * 1024
def KH(b):   return 18432 + b * 2048
def VH(b):   return 22528 + b * 2048
def CST(b):  return 26624 + b * 1024
def CTX(h):  return h * 1024
def ATB(kc): return 32768 + kc * 1024
def X1B(kc): return kc * 1024
AT0 = 32768
BT0 = 36864


def build_program(dump=False):
    nc = bass.Bass("TRN2", target_bir_lowering=False, debug=False)
    skind = {"kind": "ExternalOutput"} if dump else {}

    xT = nc.dram_tensor("xT", [D, S], F32, kind="ExternalInput").ap()
    wqT = nc.dram_tensor("wqT", [D, D], F32, kind="ExternalInput").ap()
    wkT = nc.dram_tensor("wkT", [D, D], F32, kind="ExternalInput").ap()
    wvT = nc.dram_tensor("wvT", [D, D], F32, kind="ExternalInput").ap()
    woT = nc.dram_tensor("woT", [D, D], F32, kind="ExternalInput").ap()
    wbT = nc.dram_tensor("wbT", [D, D], F32, kind="ExternalInput").ap()
    bbC = nc.dram_tensor("bbC", [128, 32], F32, kind="ExternalInput").ap()
    wrT = nc.dram_tensor("wrT", [D, 8], F32, kind="ExternalInput").ap()
    ATd = nc.dram_tensor("ATd", [D, 128], F32, kind="ExternalInput").ap()
    BTd = nc.dram_tensor("BTd", [128, D], F32, kind="ExternalInput").ap()
    identI = nc.dram_tensor("identI", [128, 128], BF16, kind="ExternalInput").ap()
    identF = nc.dram_tensor("identF", [128, 128], F32, kind="ExternalInput").ap()
    REPI = nc.dram_tensor("REPI", [8, 128], BF16, kind="ExternalInput").ap()
    yT = nc.dram_tensor("yT", [D, TOWN], F32, kind="ExternalOutput").ap()

    q_dram = nc.dram_tensor("q_dram", [32, 128, 1024], BF16, **skind).ap()
    k_dram = nc.dram_tensor("k_dram", [32, 128, 2048], BF16, **skind).ap()
    v_dram = nc.dram_tensor("v_dram", [32, 128, 2048], BF16, **skind).ap()
    ctx_dram = nc.dram_tensor("ctx_dram", [32, 128, 1024], BF16, **skind).ap()
    x1_dram = nc.dram_tensor("x1_dram", [32, 128, 1024], F32, **skind).ap()
    ffn_dram = nc.dram_tensor("ffn_dram", [32, 128, 1024], BF16, **skind).ap()
    if dump:
        xn_dram = nc.dram_tensor("xn_dram", [32, 128, 2048], BF16,
                                 kind="ExternalOutput").ap()
        xn2_dram = nc.dram_tensor("xn2_dram", [16, 128, 2048], BF16,
                                  kind="ExternalOutput").ap()

    cnt = {}

    def bump(name, n=1):
        cnt[name] = cnt.get(name, 0) + n
        return cnt[name]

    def cur(name):
        return cnt.get(name, 0)

    with contextlib.ExitStack() as stk:
        sbt = lambda n, shp, dt: stk.enter_context(nc.sbuf_tensor(n, shp, dt))
        pst = lambda n, shp: stk.enter_context(nc.psum_tensor(n, shp, F32))

        A = sbt("A", [128, 65536], BF16)
        wbf = [sbt(f"wbf{i}", [128, 4096], BF16) for i in range(2)]
        Df = sbt("Df", [128, 4096], F32)
        s_sq = sbt("s_sq", [128, 4096], BF16)
        s_qkstg = sbt("s_qkstg", [128, 2048], BF16)
        s_ev = sbt("s_ev", [128, 4096], BF16)
        s_hw = sbt("s_hw", [128, 1024], BF16)
        s_rows = sbt("s_rows", [1, 1536], F32)
        s_lg = sbt("s_lg", [8, 512], F32)
        s_wT = sbt("s_wT", [8, 1024], BF16)
        s_rc = sbt("s_rc", [1, 1024], F32)
        s_ident = sbt("s_ident", [128, 128], BF16)
        s_identf = sbt("s_identf", [128, 128], F32)
        s_rep = sbt("s_rep", [8, 128], BF16)
        s_onesc = sbt("s_onesc", [128, 1], BF16)
        s_onesr = sbt("s_onesr", [1, 128], F32)
        s_m1 = sbt("s_m1", [128, 8], F32)
        s_m2 = sbt("s_m2", [128, 8], F32)
        s_e2 = sbt("s_e2", [128, 8], F32)
        s_den = sbt("s_den", [128, 8], F32)
        s_w1 = sbt("s_w1", [128, 8], F32)
        s_w2 = sbt("s_w2", [128, 8], F32)
        s_msk1 = sbt("s_msk1", [128, 64], F32)
        s_msk2 = sbt("s_msk2", [128, 64], F32)
        s_lg2 = sbt("s_lg2", [128, 64], F32)
        s_wf = sbt("s_wf", [128, 64], F32)
        s_rt = sbt("s_rt", [128, 256], BF16)
        s_bb = sbt("s_bb", [128, 32], F32)

        pb = [pst(f"pb{i}", [128, 512]) for i in range(8)]

        sems = {n: stk.enter_context(nc.semaphore(n))
                for n in ("dmx", "dmw0", "dmw1", "dmq0", "dmq1", "dmc",
                          "dmo0", "dmo1", "pe", "sce", "vee")}

        blk = stk.enter_context(nc.Block())

        def t_mm(t, out, lhsT, rhs, start, stop):
            t.matmul(out, lhsT, rhs, start=start, stop=stop).then_inc(sems["pe"], 1)
            return bump("pe")

        def t_tr(t, out, in_, ident):
            t.transpose(out, in_, ident).then_inc(sems["pe"], 1)
            return bump("pe")

        def a_act(e, out, in_, func, bias, scale):
            e.activation(out=out, in_=in_, func=func, bias=bias,
                         scale=scale).then_inc(sems["sce"], 1)
            return bump("sce")

        def v_tt(v, out, in0, in1, op):
            v.tensor_tensor(out=out, in0=in0, in1=in1, op=op).then_inc(sems["vee"], 1)
            return bump("vee")

        def v_cp(v, out, in_):
            v.tensor_copy(out=out, in_=in_).then_inc(sems["vee"], 1)
            return bump("vee")

        def vself(v):
            v.wait_ge(sems["vee"], cur("vee"))

        def d_dma(e, sem, out, in_):
            e.dma_start(out=out, in_=in_).then_inc(sems[sem], 16)
            return bump(sem, 16)

        def waits(eng, g):
            for s, m in g.items():
                eng.wait_ge(sems[s], m)

        bankg = [dict() for _ in range(8)]

        def bank_read(i, sem, mark):
            if bankg[i].get(sem, -1) < mark:
                bankg[i][sem] = mark

        def bank_take(eng, i):
            waits(eng, bankg[i])
            bankg[i].clear()

        evg = [dict() for _ in range(8)]
        sqg = [dict() for _ in range(4)]

        wld = []
        wcons = []

        def w_load(src_col_block):
            i = len(wld)

            @blk.gpsimd
            def _(e, i=i, src=src_col_block):
                if i >= 2:
                    e.wait_ge(sems["pe"], wcons[i - 2])
                else:
                    e.wait_ge(sems["vee"], sqv[28 + i])
                dst = wbf[i % 2][:].rearrange("p (kc e) -> p kc e", e=128)
                wld.append(d_dma(e, f"dmw{i % 2}", dst,
                                 src.rearrange("(kc p) e -> p kc e", p=128)))
            return i

        def ln_rows(sumb, sqb, rb, cb, pe_ready):
            mu = s_rows[0:1, 0:512]
            ex2 = s_rows[0:1, 512:1024]
            var = s_rows[0:1, 1024:1536]

            @blk.scalar
            def _(e):
                e.wait_ge(sems["pe"], pe_ready)
                a_act(e, mu, pb[sumb][0:1, :], AF.Copy, 0.0, 1.0 / D)
                a_act(e, ex2, pb[sqb][0:1, :], AF.Copy, 0.0, 1.0 / D)
            m1 = cur("sce")
            bank_read(sumb, "sce", m1)
            bank_read(sqb, "sce", m1)

            @blk.vector
            def _(v):
                v.wait_ge(sems["sce"], m1)
                v_tt(v, var, mu, mu, ALU.mult)
                vself(v)
                v.scalar_tensor_tensor(
                    out=var, in0=ex2, scalar=LN_EPS, in1=var, op0=ALU.add,
                    op1=ALU.subtract).then_inc(sems["vee"], 1)
                bump("vee")
            m2 = cur("vee")

            @blk.scalar
            def _(e):
                e.wait_ge(sems["vee"], m2)
                a_act(e, ex2, var, AF.Sqrt, 0.0, 1.0)
            m3 = cur("sce")

            @blk.vector
            def _(v):
                v.wait_ge(sems["sce"], m3)
                v.reciprocal(out=s_rc[0:1, 0:512], in_=ex2).then_inc(
                    sems["vee"], 1)
                bump("vee")
                vself(v)
                v_tt(v, s_rc[0:1, 512:1024], mu, s_rc[0:1, 0:512], ALU.mult)
            m4 = cur("vee")

            @blk.tensor
            def _(t):
                t.wait_ge(sems["vee"], m4)
                bank_take(t, rb)
                t_mm(t, pb[rb][:, :], s_onesr[0:1, :], s_rc[0:1, 0:512], True, True)
                bank_take(t, cb)
                t_mm(t, pb[cb][:, :], s_onesr[0:1, :], s_rc[0:1, 512:1024],
                     True, True)
            return cur("pe")

        @blk.sync
        def _(e):
            d_dma(e, "dmx", s_ident[:], identI[:, :])
            d_dma(e, "dmx", s_identf[:], identF[:, :])
            d_dma(e, "dmx", s_rep[:], REPI[:, :])
        init_dmx = cur("dmx")

        @blk.gpsimd
        def _(e):
            d_dma(e, "dmc", s_rt[:].rearrange("p (kc e) -> p kc e", e=8),
                  wrT.rearrange("(kc p) e -> p kc e", p=128))
            d_dma(e, "dmc", s_bb[:], bbC[:, :])
            e.wait_ge(sems["dmc"], cur("dmc"))
        init_dmw = cur("dmc")

        @blk.vector
        def _(v):
            v.memset(s_onesc[:], 1.0).then_inc(sems["vee"], 1)
            bump("vee")
            v.memset(s_onesr[:], 1.0).then_inc(sems["vee"], 1)
            bump("vee")
        init_vee = cur("vee")

        xld, sqv, statmm = [], [], []
        for kc in range(NKC):
            XSEM = ("dmw0", "dmw1", "dmq0", "dmq1")

            @blk.gpsimd
            def _(e, kc=kc):
                if kc >= 4:
                    e.wait_ge(sems["vee"], sqv[kc - 4])
                xld.append(d_dma(e, XSEM[kc % 4], A[:, XB(kc):XB(kc) + 2048],
                                 xT[kc * 128:(kc + 1) * 128, :]))

            @blk.vector
            def _(v, kc=kc):
                v.wait_ge(sems[XSEM[kc % 4]], xld[kc])
                if kc >= 2:
                    v.wait_ge(sems["pe"], statmm[kc - 2])
                sl = (kc % 2) * 2048
                sqv.append(v_tt(v, s_sq[:, sl:sl + 2048],
                                A[:, XB(kc):XB(kc) + 2048],
                                A[:, XB(kc):XB(kc) + 2048], ALU.mult))

            @blk.tensor
            def _(t, kc=kc):
                if kc == 0:
                    t.wait_ge(sems["vee"], init_vee)
                t.wait_ge(sems["vee"], sqv[kc])
                st, sp = kc == 0, kc == NKC - 1
                sl = (kc % 2) * 2048
                for tt in range(4):
                    t_mm(t, pb[tt][0:1, :], s_onesc[:],
                         A[:, XB(kc) + tt * 512:XB(kc) + (tt + 1) * 512], st, sp)
                for tt in range(4):
                    m = t_mm(t, pb[4 + tt][0:1, :], s_onesc[:],
                             s_sq[:, sl + tt * 512:sl + (tt + 1) * 512], st, sp)
                statmm.append(m)

        ln1_rc = []
        for tt in range(4):
            ln1_rc.append(ln_rows(tt, 4 + tt, tt, 4 + tt,
                                  statmm[-1] if tt == 0 else ln1_rc[-1]))

        p2v = 0
        for kc in range(NKC):
            @blk.vector
            def _(v, kc=kc):
                if kc == 0:
                    v.wait_ge(sems["pe"], ln1_rc[-1])
                for tt in range(4):
                    tp = Df[:, (tt % 2) * 512:(tt % 2) * 512 + 512]
                    xs = A[:, XB(kc) + tt * 512:XB(kc) + (tt + 1) * 512]
                    if tt >= 2:
                        vself(v)
                    v_tt(v, tp, xs, pb[tt][:, :], ALU.mult)
                    vself(v)
                    v_tt(v, xs, tp, pb[4 + tt][:, :], ALU.subtract)
        p2v = cur("vee")
        for tt in range(4):
            bank_read(tt, "vee", p2v)
            bank_read(4 + tt, "vee", p2v)

        if dump:
            @blk.sync
            def _(e):
                for kc in range(NKC):
                    e.wait_ge(sems["vee"], p2v)
                    d_dma(e, "dmx", xn_dram[kc, :, :],
                          A[:, XB(kc):XB(kc) + 2048])

        ev_i = 0

        def evac_store(pe_mark, bank, dst):
            nonlocal ev_i
            sl = ev_i % 8
            ev_i += 1

            @blk.vector
            def _(v, sl=sl, bank=bank, pe_mark=pe_mark):
                v.wait_ge(sems["pe"], pe_mark)
                if evg[sl]:
                    v.wait_ge(sems["dmx"], cur("dmx"))
                    evg[sl].clear()
                v_cp(v, s_ev[:, sl * 512:(sl + 1) * 512], pb[bank][:, :])
            m = cur("vee")
            bank_read(bank, "vee", m)

            @blk.sync
            def _(e, sl=sl, dst=dst, m=m):
                e.wait_ge(sems["vee"], m)
                d_dma(e, "dmx", dst, s_ev[:, sl * 512:(sl + 1) * 512])
            evg[sl]["dmx"] = cur("dmx")
            return cur("dmx")

        def qk_matrix(wsrc, dram, ntt, first_extra_vee=None):
            for eb in range(32):
                i = w_load(wsrc[:, eb * 128:(eb + 1) * 128])
                last = 0
                for tt in range(ntt):
                    bank = tt % 2

                    @blk.tensor
                    def _(t, i=i, eb=eb, tt=tt, bank=bank,
                          fx=(first_extra_vee if eb == 0 and tt == 0 else None)):
                        if fx is not None:
                            t.wait_ge(sems["vee"], fx)
                        if tt == 0:
                            t.wait_ge(sems[f"dmw{i % 2}"], wld[i])
                        bank_take(t, bank)
                        for kc in range(NKC):
                            m = t_mm(t, pb[bank][:, :],
                                     wbf[i % 2][:, kc * 128:(kc + 1) * 128],
                                     A[:, XB(kc) + tt * 512:XB(kc) + (tt + 1) * 512],
                                     kc == 0, kc == NKC - 1)
                        _qk_last[0] = m
                    last = _qk_last[0]
                    evac_store(last, bank, dram[eb, :, tt * 512:(tt + 1) * 512])
                wcons.append(last)

        _qk_last = [0]
        qk_matrix(wqT, q_dram, 2, first_extra_vee=p2v)
        qk_matrix(wkT, k_dram, 4)

        for eb in range(32):
            i = w_load(wvT[:, eb * 128:(eb + 1) * 128])
            stgv = []
            for tt in range(4):
                bank = tt % 2

                @blk.tensor
                def _(t, i=i, tt=tt, bank=bank):
                    if tt == 0:
                        t.wait_ge(sems[f"dmw{i % 2}"], wld[i])
                    bank_take(t, bank)
                    for kc in range(NKC):
                        m = t_mm(t, pb[bank][:, :],
                                 wbf[i % 2][:, kc * 128:(kc + 1) * 128],
                                 A[:, XB(kc) + tt * 512:XB(kc) + (tt + 1) * 512],
                                 kc == 0, kc == NKC - 1)
                    _qk_last[0] = m
                mlast = _qk_last[0]

                @blk.vector
                def _(v, tt=tt, bank=bank, mlast=mlast, eb=eb):
                    v.wait_ge(sems["pe"], mlast)
                    if eb > 0:
                        v.wait_ge(sems["pe"], _vtr_last[0])
                    v_cp(v, Df[:, tt * 512:(tt + 1) * 512], pb[bank][:, :])
                stgv.append(cur("vee"))
                bank_read(bank, "vee", stgv[-1])
            wcons.append(mlast)
            for ktg in range(4):
                tbank = 2 + ktg % 2

                @blk.tensor
                def _(t, ktg=ktg, tbank=tbank):
                    t.wait_ge(sems["vee"], stgv[ktg])
                    bank_take(t, tbank)
                    for j in range(4):
                        kt = ktg * 4 + j
                        m = t_tr(t, pb[tbank][:, j * 128:(j + 1) * 128],
                                 Df[:, kt * 128:(kt + 1) * 128], s_identf[:])
                    _qk_last[0] = m
                _vtr_last[0] = _qk_last[0]
                evac_store(_qk_last[0], tbank,
                           v_dram[eb, :, ktg * 512:(ktg + 1) * 512])
        p3_pe = cur("pe")

        qkvld = {}
        headlast = {}
        expm = {}
        avm = {}
        ctxmul = {}
        drm = {}
        bcm = {}
        ctxdma = {}

        def load_head(e, h):
            b = h % 2
            g = {"pe": headlast[h - 2]} if h - 2 in headlast else {"pe": p3_pe}
            waits(e, g)
            d_dma(e, f"dmq{b}", A[:, QH(b):QH(b) + 1024], q_dram[h, :, :])
            d_dma(e, f"dmq{b}", A[:, KH(b):KH(b) + 2048], k_dram[h, :, :])
            qkvld[h] = d_dma(e, f"dmq{b}", A[:, VH(b):VH(b) + 2048],
                             v_dram[h, :, :])

        p3_dmx = cur("dmx")

        @blk.sync
        def _(e):
            e.wait_ge(sems["dmx"], p3_dmx)
            load_head(e, 0)
            load_head(e, 1)

        for h in range(32):
            hb = h % 2
            for qt in range(2):
                i = h * 2 + qt
                ib = i % 2

                for kt in range(16):
                    bank = kt % 2

                    @blk.tensor
                    def _(t, h=h, qt=qt, i=i, hb=hb, kt=kt, bank=bank):
                        if qt == 0 and kt == 0:
                            t.wait_ge(sems[f"dmq{h % 2}"], qkvld[h])
                        bank_take(t, bank)
                        _scm[(i, kt)] = t_mm(
                            t, pb[bank][:, :],
                            A[:, KH(hb) + kt * 128:KH(hb) + (kt + 1) * 128],
                            A[:, QH(hb) + qt * 512:QH(hb) + (qt + 1) * 512],
                            True, True)

                    @blk.scalar
                    def _(e, i=i, ib=ib, kt=kt, bank=bank):
                        e.wait_ge(sems["pe"], _scm[(i, kt)])
                        if (i - 2, kt) in avm:
                            e.wait_ge(sems["pe"], avm[(i - 2, kt)])
                        elif i < 2:
                            e.wait_ge(sems["pe"], p3_pe)
                        m = a_act(e, A[:, PT(ib) + kt * 512:PT(ib) + (kt + 1) * 512],
                                  pb[bank][:, :], AF.Exp, 0.0, ISQD)
                        expm[(i, kt)] = m
                        bank_read(bank, "sce", m)

                cb = 2 + ib
                db = 4 + ib

                @blk.tensor
                def _(t, i=i, ib=ib, hb=hb, cb=cb, db=db):
                    bank_take(t, cb)
                    bank_take(t, db)
                    for kt in range(16):
                        t.wait_ge(sems["sce"], expm[(i, kt)])
                        pt = A[:, PT(ib) + kt * 512:PT(ib) + (kt + 1) * 512]
                        m = t_mm(t, pb[cb][:, :],
                                 A[:, VH(hb) + kt * 128:VH(hb) + (kt + 1) * 128],
                                 pt, kt == 0, kt == 15)
                        avm[(i, kt)] = m
                        t_mm(t, pb[db][0:1, :], s_onesc[:], pt, kt == 0, kt == 15)
                    _scm[("den", i)] = cur("pe")

                @blk.vector
                def _(v, i=i, db=db):
                    v.wait_ge(sems["pe"], _scm[("den", i)])
                    if i - 1 in bcm:
                        v.wait_ge(sems["pe"], bcm[i - 1])
                    v_cp(v, s_rc[0:1, 0:512], pb[db][0:1, :])
                    vself(v)
                    v.reciprocal(out=s_rc[0:1, 512:1024],
                                 in_=s_rc[0:1, 0:512]).then_inc(
                        sems["vee"], 1)
                    drm[i] = bump("vee")
                    bank_read(db, "vee", drm[i])

                Db = 6 + ib

                @blk.tensor
                def _(t, i=i, Db=Db):
                    t.wait_ge(sems["vee"], drm[i])
                    bank_take(t, Db)
                    bcm[i] = t_mm(t, pb[Db][:, :], s_onesr[0:1, :],
                                  s_rc[0:1, 512:1024], True, True)

                @blk.vector
                def _(v, i=i, h=h, qt=qt, hb=hb, cb=cb, Db=Db):
                    v.wait_ge(sems["pe"], bcm[i])
                    if qt == 0 and h - 2 in ctxdma:
                        v.wait_ge(sems["dmx"], cur("dmx"))
                    cs = A[:, CST(hb) + qt * 512:CST(hb) + (qt + 1) * 512]
                    v_cp(v, cs, pb[cb][:, :])
                    vself(v)
                    ctxmul[i] = v_tt(v, cs, cs, pb[Db][:, :], ALU.mult)
                    bank_read(cb, "vee", ctxmul[i])
                    bank_read(Db, "vee", ctxmul[i])

            headlast[h] = cur("pe")

            @blk.sync
            def _(e, h=h, hb=hb):
                if h + 2 < 32:
                    load_head(e, h + 2)
                e.wait_ge(sems["vee"], ctxmul[h * 2 + 1])
                ctxdma[h] = d_dma(e, "dmx", ctx_dram[h, :, :],
                                  A[:, CST(hb):CST(hb) + 1024])

        p4_pe = cur("pe")
        p4_vee = cur("vee")

        if dump:
            @blk.sync
            def _(e):
                e.wait_ge(sems["pe"], p4_pe)
                for kc in range(16, NKC):
                    d_dma(e, "dmx", xn2_dram[kc - 16, :, :],
                          A[:, XB(kc):XB(kc) + 2048])

        @blk.sync
        def _(e):
            e.wait_ge(sems["pe"], p4_pe)
            e.wait_ge(sems["vee"], p4_vee)
            e.wait_ge(sems["dmx"], ctxdma[31])
            d_dma(e, "dmx", A[:, 0:32768].rearrange("p (h t) -> p h t", t=1024),
                  ctx_dram.rearrange("h p t -> p h t"))
        ctxrb = cur("dmx")

        atv = {}
        for ob in range(32):
            i = w_load(woT[:, ob * 128:(ob + 1) * 128])
            for qt in range(2):
                bank = (ob * 2 + qt) % 2

                @blk.tensor
                def _(t, i=i, ob=ob, qt=qt, bank=bank):
                    if ob == 0 and qt == 0:
                        t.wait_ge(sems["dmx"], ctxrb)
                    if qt == 0:
                        t.wait_ge(sems[f"dmw{i % 2}"], wld[i])
                    bank_take(t, bank)
                    for kc in range(NKC):
                        m = t_mm(t, pb[bank][:, :],
                                 wbf[i % 2][:, kc * 128:(kc + 1) * 128],
                                 A[:, CTX(kc) + qt * 512:CTX(kc) + (qt + 1) * 512],
                                 kc == 0, kc == NKC - 1)
                    _qk_last[0] = m
                mlast = _qk_last[0]
                if qt == 1:
                    wcons.append(mlast)
                sqs = (ob * 2 + qt) % 4

                @blk.vector
                def _(v, ob=ob, qt=qt, bank=bank, mlast=mlast, sqs=sqs):
                    v.wait_ge(sems["pe"], mlast)
                    waits(v, sqg[sqs])
                    sqg[sqs].clear()
                    dst = A[:, ATB(ob) + qt * 512:ATB(ob) + (qt + 1) * 512]
                    v_cp(v, dst, pb[bank][:, :])
                    vself(v)
                    atv[(ob, qt)] = v_tt(v, s_sq[:, sqs * 1024:sqs * 1024 + 512],
                                         dst, dst, ALU.mult)
                bank_read(bank, "vee", atv[(ob, qt)])

                @blk.tensor
                def _(t, ob=ob, qt=qt, sqs=sqs):
                    t.wait_ge(sems["vee"], atv[(ob, qt)])
                    st, sp = ob == 0, ob == 31
                    dst = A[:, ATB(ob) + qt * 512:ATB(ob) + (qt + 1) * 512]
                    if st:
                        bank_take(t, 2 + qt)
                        bank_take(t, 4 + qt)
                    t_mm(t, pb[2 + qt][0:1, :], s_onesc[:], dst, st, sp)
                    m = t_mm(t, pb[4 + qt][0:1, :], s_onesc[:],
                             s_sq[:, sqs * 1024:sqs * 1024 + 512], st, sp)
                    sqg[sqs]["pe"] = m

        ln2_rc = []
        for qt in range(2):
            ln2_rc.append(ln_rows(2 + qt, 4 + qt, 2 + qt, 4 + qt,
                                  cur("pe") if qt == 0 else ln2_rc[-1]))

        xol, x1wr, p6v = {}, {}, {}
        for kc in range(NKC):
            @blk.sync
            def _(e, kc=kc):
                if kc >= 2:
                    e.wait_ge(sems["vee"], p6v[kc - 2])
                xol[kc] = d_dma(e, "dmx",
                                Df[:, (kc % 2) * 1024:(kc % 2) * 1024 + 1024],
                                xT[kc * 128:(kc + 1) * 128, 0:1024])

            sqs = kc % 4

            @blk.vector
            def _(v, kc=kc, sqs=sqs):
                v.wait_ge(sems["dmx"], cur("dmx"))
                if kc == 0:
                    v.wait_ge(sems["pe"], ln2_rc[-1])
                waits(v, sqg[sqs])
                sqg[sqs].clear()
                for qt in range(2):
                    xt = Df[:, 2048 + (kc % 2) * 1024 + qt * 512:
                            2048 + (kc % 2) * 1024 + (qt + 1) * 512]
                    ats = A[:, ATB(kc) + qt * 512:ATB(kc) + (qt + 1) * 512]
                    v_tt(v, xt, ats, pb[2 + qt][:, :], ALU.mult)
                    vself(v)
                    v_tt(v, xt, xt, pb[4 + qt][:, :], ALU.subtract)
                    vself(v)
                    v_tt(v, xt, xt,
                         Df[:, (kc % 2) * 1024 + qt * 512:
                             (kc % 2) * 1024 + (qt + 1) * 512], ALU.add)
                    vself(v)
                _x1m[0] = cur("vee")
                for qt in range(2):
                    xt = Df[:, 2048 + (kc % 2) * 1024 + qt * 512:
                            2048 + (kc % 2) * 1024 + (qt + 1) * 512]
                    x1s = A[:, X1B(kc) + qt * 512:X1B(kc) + (qt + 1) * 512]
                    v_cp(v, x1s, xt)
                    vself(v)
                    v_tt(v, s_sq[:, sqs * 1024 + qt * 512:
                                 sqs * 1024 + (qt + 1) * 512],
                         x1s, x1s, ALU.mult)
                p6v[kc] = cur("vee")
            m_x1 = _x1m[0]

            @blk.sync
            def _(e, kc=kc, m_x1=m_x1):
                e.wait_ge(sems["vee"], m_x1)
                x1wr[kc] = d_dma(e, "dmx", x1_dram[kc, :, :],
                                 Df[:, 2048 + (kc % 2) * 1024:
                                     2048 + (kc % 2) * 1024 + 1024])

            @blk.tensor
            def _(t, kc=kc, sqs=sqs):
                t.wait_ge(sems["vee"], p6v[kc])
                st, sp = kc == 0, kc == 31
                if st:
                    for qt in range(2):
                        bank_take(t, 0 + qt)
                        bank_take(t, 6 + qt)
                for qt in range(2):
                    x1s = A[:, X1B(kc) + qt * 512:X1B(kc) + (qt + 1) * 512]
                    t_mm(t, pb[0 + qt][0:1, :], s_onesc[:], x1s, st, sp)
                    m = t_mm(t, pb[6 + qt][0:1, :], s_onesc[:],
                             s_sq[:, sqs * 1024 + qt * 512:
                                  sqs * 1024 + (qt + 1) * 512], st, sp)
                    sqg[sqs]["pe"] = m

        ln3_rc = []
        for qt in range(2):
            ln3_rc.append(ln_rows(0 + qt, 6 + qt, 0 + qt, 6 + qt,
                                  cur("pe") if qt == 0 else ln3_rc[-1]))

        p7av = 0
        for kc in range(NKC):
            @blk.vector
            def _(v, kc=kc):
                if kc == 0:
                    v.wait_ge(sems["pe"], ln3_rc[-1])
                    v.wait_ge(sems["dmx"], cur("dmx"))
                for qt in range(2):
                    tp = Df[:, 2048 + qt * 512:2048 + (qt + 1) * 512]
                    x1s = A[:, X1B(kc) + qt * 512:X1B(kc) + (qt + 1) * 512]
                    vself(v)
                    v_tt(v, tp, x1s, pb[0 + qt][:, :], ALU.mult)
                    vself(v)
                    v_tt(v, x1s, tp, pb[6 + qt][:, :], ALU.subtract)
        p7av = cur("vee")
        for qt in range(2):
            bank_read(0 + qt, "vee", p7av)
            bank_read(6 + qt, "vee", p7av)

        @blk.gpsimd
        def _(e):
            e.wait_ge(sems["vee"], p7av)
            d_dma(e, "dmc", A[:, AT0:AT0 + 4096].rearrange(
                "p (kc e) -> p kc e", e=128),
                ATd.rearrange("(kc p) e -> p kc e", p=128))
            d_dma(e, "dmc", A[:, BT0:BT0 + 4096], BTd[:, :])
        abld = cur("dmc")

        lgmm, wexpm, htm, hwv = {}, {}, {}, {}
        for qt in range(2):
            @blk.tensor
            def _(t, qt=qt):
                if qt == 0:
                    t.wait_ge(sems["vee"], p7av)
                    t.wait_ge(sems["dmc"], init_dmw)
                bank_take(t, 2 + qt)
                for kc in range(NKC):
                    m = t_mm(t, pb[2 + qt][0:8, :],
                             s_rt[:, kc * 8:(kc + 1) * 8],
                             A[:, X1B(kc) + qt * 512:X1B(kc) + (qt + 1) * 512],
                             kc == 0, kc == NKC - 1)
                lgmm[qt] = m

            @blk.vector
            def _(v, qt=qt):
                v.wait_ge(sems["pe"], lgmm[qt])
                if qt == 1:
                    v.wait_ge(sems["pe"], _lgtr[3])
                v_cp(v, s_lg[0:8, :], pb[2 + qt][0:8, :])
            lgev = cur("vee")
            bank_read(2 + qt, "vee", lgev)

            @blk.tensor
            def _(t, qt=qt, lgev=lgev):
                t.wait_ge(sems["vee"], lgev)
                if qt == 0:
                    t.wait_ge(sems["dmx"], init_dmx)
                    bank_take(t, 4)
                for c in range(4):
                    _lgtr[c] = t_tr(t, pb[4][:, (qt * 4 + c) * 8:(qt * 4 + c + 1) * 8],
                                    s_lg[0:8, c * 128:(c + 1) * 128],
                                    s_identf[0:8, 0:8])

            @blk.vector
            def _(v, qt=qt):
                for c in range(4):
                    cc = qt * 4 + c
                    v.wait_ge(sems["pe"], _lgtr[c])
                    lgs = pb[4][:, cc * 8:(cc + 1) * 8]
                    v.reduce_max(out=s_m1[:, cc:cc + 1], in_=lgs,
                                 axis=AX.X).then_inc(sems["vee"], 1)
                    bump("vee")
                    vself(v)
                    v.tensor_single_scalar(
                        out=s_msk1[:, cc * 8:(cc + 1) * 8], in_=lgs,
                        scalar=s_m1[:, cc:cc + 1], op=ALU.is_ge).then_inc(
                        sems["vee"], 1)
                    bump("vee")
                    vself(v)
                    v.scalar_tensor_tensor(
                        out=s_lg2[:, cc * 8:(cc + 1) * 8],
                        in0=s_msk1[:, cc * 8:(cc + 1) * 8], scalar=-1e30,
                        in1=lgs, op0=ALU.mult, op1=ALU.add).then_inc(
                        sems["vee"], 1)
                    bump("vee")
                    vself(v)
                    v.reduce_max(out=s_m2[:, cc:cc + 1],
                                 in_=s_lg2[:, cc * 8:(cc + 1) * 8],
                                 axis=AX.X).then_inc(sems["vee"], 1)
                    bump("vee")
                    vself(v)
                    v.tensor_single_scalar(
                        out=s_msk2[:, cc * 8:(cc + 1) * 8],
                        in_=s_lg2[:, cc * 8:(cc + 1) * 8],
                        scalar=s_m2[:, cc:cc + 1], op=ALU.is_ge).then_inc(
                        sems["vee"], 1)
                    bump("vee")
            _tk[qt] = cur("vee")

        @blk.vector
        def _(v):
            v.wait_ge(sems["vee"], _tk[1])
            vself(v)
            v_tt(v, s_den[:, :], s_m2[:, :], s_m1[:, :], ALU.subtract)
        tkd = cur("vee")

        @blk.scalar
        def _(e):
            e.wait_ge(sems["vee"], tkd)
            a_act(e, s_e2[:, :], s_den[:, :], AF.Exp, 0.0, 1.0)
        tke = cur("sce")

        @blk.vector
        def _(v):
            v.wait_ge(sems["sce"], tke)
            v.tensor_scalar(out=s_den[:, :], in0=s_e2[:, :], scalar1=1.0,
                            scalar2=0.0, op0=ALU.add,
                            op1=ALU.add).then_inc(sems["vee"], 1)
            bump("vee")
            vself(v)
            v.reciprocal(out=s_w2[:, :], in_=s_den[:, :]).then_inc(sems["vee"], 1)
            bump("vee")
            vself(v)
            v_tt(v, s_w2[:, :], s_w2[:, :], s_e2[:, :], ALU.mult)
            vself(v)
            v.tensor_scalar(out=s_w1[:, :], in0=s_w2[:, :], scalar1=-1.0,
                            scalar2=1.0, op0=ALU.mult, op1=ALU.add).then_inc(
                sems["vee"], 1)
            bump("vee")
            for cc in range(8):
                vself(v)
                v.tensor_single_scalar(
                    out=s_wf[:, cc * 8:(cc + 1) * 8],
                    in_=s_msk1[:, cc * 8:(cc + 1) * 8],
                    scalar=s_w1[:, cc:cc + 1], op=ALU.mult).then_inc(
                    sems["vee"], 1)
                bump("vee")
                v.tensor_single_scalar(
                    out=s_msk2[:, cc * 8:(cc + 1) * 8],
                    in_=s_msk2[:, cc * 8:(cc + 1) * 8],
                    scalar=s_w2[:, cc:cc + 1], op=ALU.mult).then_inc(
                    sems["vee"], 1)
                bump("vee")
                vself(v)
                v_tt(v, s_wf[:, cc * 8:(cc + 1) * 8],
                     s_wf[:, cc * 8:(cc + 1) * 8],
                     s_msk2[:, cc * 8:(cc + 1) * 8], ALU.add)

        tkw = cur("vee")
        bank_read(4, "vee", tkw)

        @blk.tensor
        def _(t):
            t.wait_ge(sems["vee"], tkw)
            bank_take(t, 5)
            bank_take(t, 6)
            for cc in range(8):
                bk = 5 + cc // 4
                _lgtr[cc % 4] = t_tr(t, pb[bk][0:8, (cc % 4) * 128:
                                               (cc % 4 + 1) * 128],
                                     s_wf[:, cc * 8:(cc + 1) * 8], s_identf[:])
        wtr = cur("pe")

        @blk.vector
        def _(v):
            v.wait_ge(sems["pe"], wtr)
            v_cp(v, s_wT[0:8, 0:512], pb[5][0:8, :])
            v_cp(v, s_wT[0:8, 512:1024], pb[6][0:8, :])
        wtev = cur("vee")
        bank_read(5, "vee", wtev)
        bank_read(6, "vee", wtev)

        for qt in range(2):
            @blk.tensor
            def _(t, qt=qt):
                if qt == 0:
                    t.wait_ge(sems["vee"], wtev)
                bank_take(t, 0 + qt)
                wexpm[qt] = t_mm(t, pb[0 + qt][:, :], s_rep[0:8, :],
                                 s_wT[0:8, qt * 512:(qt + 1) * 512], True, True)
                if qt == 0:
                    t.wait_ge(sems["dmc"], abld)
                bank_take(t, 6 + qt)
                for kc in range(NKC):
                    m = t_mm(t, pb[6 + qt][:, :],
                             A[:, AT0 + kc * 128:AT0 + (kc + 1) * 128],
                             A[:, X1B(kc) + qt * 512:X1B(kc) + (qt + 1) * 512],
                             kc == 0, kc == NKC - 1)
                htm[qt] = m

            @blk.vector
            def _(v, qt=qt):
                v.wait_ge(sems["pe"], htm[qt])
                hs = s_hw[:, qt * 512:(qt + 1) * 512]
                v_cp(v, hs, pb[6 + qt][:, :])
                vself(v)
                hwv[qt] = v_tt(v, hs, hs, pb[0 + qt][:, :], ALU.mult)
            bank_read(0 + qt, "vee", hwv[qt])
            bank_read(6 + qt, "vee", hwv[qt])

        ffv = {}
        for ob in range(32):
            i = w_load(wbT[:, ob * 128:(ob + 1) * 128])
            for qt in range(2):
                bank = 2 + (ob * 2 + qt) % 2

                @blk.tensor
                def _(t, i=i, ob=ob, qt=qt, bank=bank):
                    if ob == 0 and qt == 0:
                        t.wait_ge(sems["vee"], hwv[1])
                    if qt == 0:
                        t.wait_ge(sems[f"dmw{i % 2}"], wld[i])
                    bank_take(t, bank)
                    for kc in range(NKC):
                        t_mm(t, pb[bank][:, :],
                             wbf[i % 2][:, kc * 128:(kc + 1) * 128],
                             A[:, X1B(kc) + qt * 512:X1B(kc) + (qt + 1) * 512],
                             kc == 0, False)
                    m = t_mm(t, pb[bank][:, :],
                             A[:, BT0 + ob * 128:BT0 + (ob + 1) * 128],
                             s_hw[:, qt * 512:(qt + 1) * 512], False, True)
                    _qk_last[0] = m
                mlast = _qk_last[0]
                if qt == 1:
                    wcons.append(mlast)
                sl = (ob * 2 + qt) % 8
                sqs = (ob * 2 + qt) % 4

                @blk.vector
                def _(v, ob=ob, mlast=mlast, bank=bank, sl=sl, sqs=sqs):
                    v.wait_ge(sems["pe"], mlast)
                    if "dmx" in evg[sl]:
                        v.wait_ge(sems["dmx"], cur("dmx"))
                        del evg[sl]["dmx"]
                    waits(v, evg[sl])
                    evg[sl].clear()
                    waits(v, sqg[sqs])
                    sqg[sqs].clear()
                    v.tensor_single_scalar(
                        out=s_ev[:, sl * 512:(sl + 1) * 512],
                        in_=pb[bank][:, :], scalar=s_bb[:, ob:ob + 1],
                        op=ALU.add).then_inc(sems["vee"], 1)
                    bump("vee")
                    vself(v)
                    v_tt(v, s_sq[:, sqs * 1024:sqs * 1024 + 512],
                         s_ev[:, sl * 512:(sl + 1) * 512],
                         s_ev[:, sl * 512:(sl + 1) * 512], ALU.mult)
                ffv[(ob, qt)] = cur("vee")
                bank_read(bank, "vee", ffv[(ob, qt)])

                @blk.sync
                def _(e, ob=ob, qt=qt, sl=sl):
                    e.wait_ge(sems["vee"], ffv[(ob, qt)])
                    d_dma(e, "dmx", ffn_dram[ob, :, qt * 512:(qt + 1) * 512],
                          s_ev[:, sl * 512:(sl + 1) * 512])
                evg[sl]["dmx"] = cur("dmx")

                @blk.tensor
                def _(t, ob=ob, qt=qt, sl=sl, sqs=sqs):
                    t.wait_ge(sems["vee"], ffv[(ob, qt)])
                    st, sp = ob == 0, ob == 31
                    if st:
                        bank_take(t, 4 + qt)
                        bank_take(t, 0 + qt)
                    t_mm(t, pb[4 + qt][0:1, :], s_onesc[:],
                         s_ev[:, sl * 512:(sl + 1) * 512], st, sp)
                    m = t_mm(t, pb[0 + qt][0:1, :], s_onesc[:],
                             s_sq[:, sqs * 1024:sqs * 1024 + 512], st, sp)
                    evg[sl]["pe"] = m
                    sqg[sqs]["pe"] = m

        ln4_rc = []
        for qt in range(2):
            ln4_rc.append(ln_rows(4 + qt, 0 + qt, 4 + qt, 0 + qt,
                                  cur("pe") if qt == 0 else ln4_rc[-1]))

        p8_pre = cur("dmx")
        p8ld, p8v, ydma = {}, {}, {}
        for kc in range(NKC):
            @blk.sync
            def _(e, kc=kc):
                if kc == 0:
                    e.wait_ge(sems["dmx"], p8_pre)
                if kc >= 2:
                    e.wait_ge(sems["vee"], p8v[kc - 2])
                d_dma(e, "dmx", Df[:, (kc % 2) * 1024:(kc % 2) * 1024 + 1024],
                      x1_dram[kc, :, :])
                p8ld[kc] = d_dma(e, "dmx",
                                 s_qkstg[:, (kc % 2) * 1024:(kc % 2) * 1024 + 1024],
                                 ffn_dram[kc, :, :])

            @blk.vector
            def _(v, kc=kc):
                v.wait_ge(sems["dmx"], cur("dmx"))
                if kc == 0:
                    v.wait_ge(sems["pe"], ln4_rc[-1])
                if kc - 2 in ydma:
                    v.wait_ge(sems[f"dmo{kc % 2}"], ydma[kc - 2])
                for qt in range(2):
                    ys = Df[:, 2048 + (kc % 2) * 1024 + qt * 512:
                            2048 + (kc % 2) * 1024 + (qt + 1) * 512]
                    fs = s_qkstg[:, (kc % 2) * 1024 + qt * 512:
                                 (kc % 2) * 1024 + (qt + 1) * 512]
                    v_tt(v, ys, fs, pb[4 + qt][:, :], ALU.mult)
                    vself(v)
                    v_tt(v, ys, ys, pb[0 + qt][:, :], ALU.subtract)
                    vself(v)
                    v_tt(v, ys, ys,
                         Df[:, (kc % 2) * 1024 + qt * 512:
                             (kc % 2) * 1024 + (qt + 1) * 512], ALU.add)
                p8v[kc] = cur("vee")

            @blk.sync
            def _(e, kc=kc):
                e.wait_ge(sems["vee"], p8v[kc])
                ydma[kc] = d_dma(e, f"dmo{kc % 2}", yT[kc * 128:(kc + 1) * 128, :],
                                 Df[:, 2048 + (kc % 2) * 1024:
                                     2048 + (kc % 2) * 1024 + 1024])

        @blk.sync
        def _(e):
            e.wait_ge(sems["dmo0"], ydma[30])
            e.wait_ge(sems["dmo1"], ydma[31])

    return nc


_scm = {}
_lgtr = {}
_tk = {}
_vtr_last = [0]
_x1m = [0]
_x1m_frozen = {}
_p6pe = {}

_NC = None


def _get_nc():
    global _NC
    if _NC is None:
        _scm.clear()
        _lgtr.clear()
        _tk.clear()
        _NC = build_program()
    return _NC


def kernel(**inputs):
    x = np.asarray(inputs["L0_attn_x"], np.float32)
    w_q = np.asarray(inputs["L0_attn_w_q"], np.float32)
    w_k = np.asarray(inputs["L0_attn_w_k"], np.float32)
    w_v = np.asarray(inputs["L0_attn_w_v"], np.float32)
    w_o = np.asarray(inputs["L0_attn_w_o"], np.float32)
    W_base = np.asarray(inputs["L0_ffn_W_base"], np.float32)
    b_base = np.asarray(inputs["L0_ffn_b_base"], np.float32)
    W_router = np.asarray(inputs["L0_ffn_W_router"], np.float32)
    Am = np.asarray(inputs["L0_ffn_A"], np.float32)
    Bm = np.asarray(inputs["L0_ffn_B"], np.float32)

    shared = {
        "wqT": np.ascontiguousarray(w_q.T),
        "wkT": np.ascontiguousarray(w_k.T),
        "wvT": np.ascontiguousarray(w_v.T),
        "woT": np.ascontiguousarray(w_o.T),
        "wbT": np.ascontiguousarray(W_base.T),
        "bbC": np.ascontiguousarray(b_base.reshape(32, 128).T),
        "wrT": np.ascontiguousarray(W_router.T),
        "ATd": np.ascontiguousarray(Am.transpose(2, 0, 1).reshape(D, 128)),
        "BTd": np.ascontiguousarray(Bm.transpose(0, 2, 1).reshape(128, D)),
        "identI": np.eye(128, dtype=ml_dtypes.bfloat16),
        "identF": np.eye(128, dtype=np.float32),
        "REPI": (2.0 * (np.arange(128)[None, :] // 16 ==
                        np.arange(8)[:, None])).astype(ml_dtypes.bfloat16),
    }
    in_maps = []
    for c in range(8):
        b, half = c // 2, c % 2
        own = x[b, half * 1024:(half + 1) * 1024].T
        oth = x[b, (1 - half) * 1024:(2 - half) * 1024].T
        m = dict(shared)
        m["xT"] = np.ascontiguousarray(np.concatenate([own, oth], axis=1))
        in_maps.append(m)

    nc = _get_nc()
    res = run_bass_kernel_spmd(nc, in_maps, core_ids=list(range(8)))
    out = np.empty((4, 2048, D), np.float32)
    for c in range(8):
        b, half = c // 2, c % 2
        out[b, half * 1024:(half + 1) * 1024, :] = res.results[c]["yT"].T
    return out


if __name__ == "__main__":
    nc = _get_nc()
    print("build ok")


# version 12
# speedup vs baseline: 1.1556x; 1.0229x over previous
import contextlib
import numpy as np
import ml_dtypes
import concourse.bass as bass
import concourse.mybir as mybir
from concourse.bass_utils import run_bass_kernel_spmd

F32 = mybir.dt.float32
BF16 = mybir.dt.bfloat16
AF = mybir.ActivationFunctionType
ALU = mybir.AluOpType
AX = mybir.AxisListType

D = 4096
S = 2048
TOWN = 1024
NKC = 32
ISQD = 0.08838834764831845
LN_EPS = 1e-5

def XB(kc):  return kc * 2048
def PT(b):   return b * 8192
def QH(b):   return 16384 + b * 1024
def KH(b):   return 18432 + b * 2048
def VH(b):   return 22528 + b * 2048
def CST(b):  return 26624 + b * 1024
def CTX(h):  return h * 1024
def ATB(kc): return 32768 + kc * 1024
def X1B(kc): return kc * 1024
AT0 = 32768
BT0 = 36864


def build_program(dump=False):
    nc = bass.Bass("TRN2", target_bir_lowering=False, debug=False)
    skind = {"kind": "ExternalOutput"} if dump else {}

    xT = nc.dram_tensor("xT", [D, S], F32, kind="ExternalInput").ap()
    wqT = nc.dram_tensor("wqT", [D, D], F32, kind="ExternalInput").ap()
    wkT = nc.dram_tensor("wkT", [D, D], F32, kind="ExternalInput").ap()
    wvT = nc.dram_tensor("wvT", [D, D], F32, kind="ExternalInput").ap()
    woT = nc.dram_tensor("woT", [D, D], F32, kind="ExternalInput").ap()
    wbT = nc.dram_tensor("wbT", [D, D], F32, kind="ExternalInput").ap()
    bbC = nc.dram_tensor("bbC", [128, 32], F32, kind="ExternalInput").ap()
    wrT = nc.dram_tensor("wrT", [D, 8], F32, kind="ExternalInput").ap()
    ATd = nc.dram_tensor("ATd", [D, 128], F32, kind="ExternalInput").ap()
    BTd = nc.dram_tensor("BTd", [128, D], F32, kind="ExternalInput").ap()
    identI = nc.dram_tensor("identI", [128, 128], BF16, kind="ExternalInput").ap()
    identF = nc.dram_tensor("identF", [128, 128], F32, kind="ExternalInput").ap()
    REPI = nc.dram_tensor("REPI", [8, 128], BF16, kind="ExternalInput").ap()
    yT = nc.dram_tensor("yT", [D, TOWN], F32, kind="ExternalOutput").ap()

    q_dram = nc.dram_tensor("q_dram", [32, 128, 1024], BF16, **skind).ap()
    k_dram = nc.dram_tensor("k_dram", [32, 128, 2048], BF16, **skind).ap()
    v_dram = nc.dram_tensor("v_dram", [32, 128, 2048], BF16, **skind).ap()
    ctx_dram = nc.dram_tensor("ctx_dram", [32, 128, 1024], BF16, **skind).ap()
    x1_dram = nc.dram_tensor("x1_dram", [32, 128, 1024], F32, **skind).ap()
    ffn_dram = nc.dram_tensor("ffn_dram", [32, 128, 1024], BF16, **skind).ap()
    if dump:
        xn_dram = nc.dram_tensor("xn_dram", [32, 128, 2048], BF16,
                                 kind="ExternalOutput").ap()
        xn2_dram = nc.dram_tensor("xn2_dram", [16, 128, 2048], BF16,
                                  kind="ExternalOutput").ap()

    cnt = {}

    def bump(name, n=1):
        cnt[name] = cnt.get(name, 0) + n
        return cnt[name]

    def cur(name):
        return cnt.get(name, 0)

    with contextlib.ExitStack() as stk:
        sbt = lambda n, shp, dt: stk.enter_context(nc.sbuf_tensor(n, shp, dt))
        pst = lambda n, shp: stk.enter_context(nc.psum_tensor(n, shp, F32))

        A = sbt("A", [128, 65536], BF16)
        wbf = [sbt(f"wbf{i}", [128, 4096], BF16) for i in range(2)]
        Df = sbt("Df", [128, 4096], F32)
        s_sq = sbt("s_sq", [128, 4096], BF16)
        s_qkstg = sbt("s_qkstg", [128, 2048], BF16)
        s_ev = sbt("s_ev", [128, 4096], BF16)
        s_hw = sbt("s_hw", [128, 1024], BF16)
        s_rows = sbt("s_rows", [1, 1536], F32)
        s_lg = sbt("s_lg", [8, 512], F32)
        s_wT = sbt("s_wT", [8, 1024], BF16)
        s_rc = sbt("s_rc", [1, 1024], F32)
        s_ident = sbt("s_ident", [128, 128], BF16)
        s_identf = sbt("s_identf", [128, 128], F32)
        s_rep = sbt("s_rep", [8, 128], BF16)
        s_onesc = sbt("s_onesc", [128, 1], BF16)
        s_onesr = sbt("s_onesr", [1, 128], F32)
        s_m1 = sbt("s_m1", [128, 8], F32)
        s_m2 = sbt("s_m2", [128, 8], F32)
        s_e2 = sbt("s_e2", [128, 8], F32)
        s_den = sbt("s_den", [128, 8], F32)
        s_w1 = sbt("s_w1", [128, 8], F32)
        s_w2 = sbt("s_w2", [128, 8], F32)
        s_msk1 = sbt("s_msk1", [128, 64], F32)
        s_msk2 = sbt("s_msk2", [128, 64], F32)
        s_lg2 = sbt("s_lg2", [128, 64], F32)
        s_wf = sbt("s_wf", [128, 64], F32)
        s_rt = sbt("s_rt", [128, 256], BF16)
        s_bb = sbt("s_bb", [128, 32], F32)

        pb = [pst(f"pb{i}", [128, 512]) for i in range(8)]

        sems = {n: stk.enter_context(nc.semaphore(n))
                for n in ("dmx", "dmw0", "dmw1", "dmq0", "dmq1", "dmc",
                          "dmo0", "dmo1", "pe", "sce", "vee")}

        blk = stk.enter_context(nc.Block())

        def t_mm(t, out, lhsT, rhs, start, stop):
            t.matmul(out, lhsT, rhs, start=start, stop=stop).then_inc(sems["pe"], 1)
            return bump("pe")

        def t_tr(t, out, in_, ident):
            t.transpose(out, in_, ident).then_inc(sems["pe"], 1)
            return bump("pe")

        def a_act(e, out, in_, func, bias, scale):
            e.activation(out=out, in_=in_, func=func, bias=bias,
                         scale=scale).then_inc(sems["sce"], 1)
            return bump("sce")

        def v_tt(v, out, in0, in1, op):
            v.tensor_tensor(out=out, in0=in0, in1=in1, op=op).then_inc(sems["vee"], 1)
            return bump("vee")

        def v_cp(v, out, in_):
            v.tensor_copy(out=out, in_=in_).then_inc(sems["vee"], 1)
            return bump("vee")

        def vself(v):
            v.wait_ge(sems["vee"], cur("vee"))

        def d_dma(e, sem, out, in_):
            e.dma_start(out=out, in_=in_).then_inc(sems[sem], 16)
            return bump(sem, 16)

        def waits(eng, g):
            for s, m in g.items():
                eng.wait_ge(sems[s], m)

        bankg = [dict() for _ in range(8)]

        def bank_read(i, sem, mark):
            if bankg[i].get(sem, -1) < mark:
                bankg[i][sem] = mark

        def bank_take(eng, i):
            waits(eng, bankg[i])
            bankg[i].clear()

        evg = [dict() for _ in range(8)]
        sqg = [dict() for _ in range(4)]

        wld = []
        wcons = []

        def w_load(src_col_block):
            i = len(wld)

            @blk.gpsimd
            def _(e, i=i, src=src_col_block):
                if i >= 2:
                    e.wait_ge(sems["pe"], wcons[i - 2])
                else:
                    e.wait_ge(sems["vee"], sqv[28 + i])
                dst = wbf[i % 2][:].rearrange("p (kc e) -> p kc e", e=128)
                wld.append(d_dma(e, f"dmw{i % 2}", dst,
                                 src.rearrange("(kc p) e -> p kc e", p=128)))
            return i

        def ln_rows(sumb, sqb, rb, cb, pe_ready):
            mu = s_rows[0:1, 0:512]
            ex2 = s_rows[0:1, 512:1024]
            var = s_rows[0:1, 1024:1536]

            @blk.scalar
            def _(e):
                e.wait_ge(sems["pe"], pe_ready)
                a_act(e, mu, pb[sumb][0:1, :], AF.Copy, 0.0, 1.0 / D)
                a_act(e, ex2, pb[sqb][0:1, :], AF.Copy, 0.0, 1.0 / D)
            m1 = cur("sce")
            bank_read(sumb, "sce", m1)
            bank_read(sqb, "sce", m1)

            @blk.vector
            def _(v):
                v.wait_ge(sems["sce"], m1)
                v_tt(v, var, mu, mu, ALU.mult)
                vself(v)
                v.scalar_tensor_tensor(
                    out=var, in0=ex2, scalar=LN_EPS, in1=var, op0=ALU.add,
                    op1=ALU.subtract).then_inc(sems["vee"], 1)
                bump("vee")
            m2 = cur("vee")

            @blk.scalar
            def _(e):
                e.wait_ge(sems["vee"], m2)
                a_act(e, ex2, var, AF.Sqrt, 0.0, 1.0)
            m3 = cur("sce")

            @blk.vector
            def _(v):
                v.wait_ge(sems["sce"], m3)
                v.reciprocal(out=s_rc[0:1, 0:512], in_=ex2).then_inc(
                    sems["vee"], 1)
                bump("vee")
                vself(v)
                v_tt(v, s_rc[0:1, 512:1024], mu, s_rc[0:1, 0:512], ALU.mult)
            m4 = cur("vee")

            @blk.tensor
            def _(t):
                t.wait_ge(sems["vee"], m4)
                bank_take(t, rb)
                t_mm(t, pb[rb][:, :], s_onesr[0:1, :], s_rc[0:1, 0:512], True, True)
                bank_take(t, cb)
                t_mm(t, pb[cb][:, :], s_onesr[0:1, :], s_rc[0:1, 512:1024],
                     True, True)
            return cur("pe")

        @blk.sync
        def _(e):
            d_dma(e, "dmx", s_ident[:], identI[:, :])
            d_dma(e, "dmx", s_identf[:], identF[:, :])
            d_dma(e, "dmx", s_rep[:], REPI[:, :])
        init_dmx = cur("dmx")

        @blk.gpsimd
        def _(e):
            d_dma(e, "dmc", s_rt[:].rearrange("p (kc e) -> p kc e", e=8),
                  wrT.rearrange("(kc p) e -> p kc e", p=128))
            d_dma(e, "dmc", s_bb[:], bbC[:, :])
            e.wait_ge(sems["dmc"], cur("dmc"))
        init_dmw = cur("dmc")

        @blk.vector
        def _(v):
            v.memset(s_onesc[:], 1.0).then_inc(sems["vee"], 1)
            bump("vee")
            v.memset(s_onesr[:], 1.0).then_inc(sems["vee"], 1)
            bump("vee")
        init_vee = cur("vee")

        xld, sqv, statmm = [], [], []
        for kc in range(NKC):
            XSEM = ("dmw0", "dmw1", "dmq0", "dmq1")

            @blk.gpsimd
            def _(e, kc=kc):
                if kc >= 4:
                    e.wait_ge(sems["vee"], sqv[kc - 4])
                xld.append(d_dma(e, XSEM[kc % 4], A[:, XB(kc):XB(kc) + 2048],
                                 xT[kc * 128:(kc + 1) * 128, :]))

            @blk.vector
            def _(v, kc=kc):
                v.wait_ge(sems[XSEM[kc % 4]], xld[kc])
                if kc >= 2:
                    v.wait_ge(sems["pe"], statmm[kc - 2])
                sl = (kc % 2) * 2048
                sqv.append(v_tt(v, s_sq[:, sl:sl + 2048],
                                A[:, XB(kc):XB(kc) + 2048],
                                A[:, XB(kc):XB(kc) + 2048], ALU.mult))

            @blk.tensor
            def _(t, kc=kc):
                if kc == 0:
                    t.wait_ge(sems["vee"], init_vee)
                t.wait_ge(sems["vee"], sqv[kc])
                st, sp = kc == 0, kc == NKC - 1
                sl = (kc % 2) * 2048
                for tt in range(4):
                    t_mm(t, pb[tt][0:1, :], s_onesc[:],
                         A[:, XB(kc) + tt * 512:XB(kc) + (tt + 1) * 512], st, sp)
                for tt in range(4):
                    m = t_mm(t, pb[4 + tt][0:1, :], s_onesc[:],
                             s_sq[:, sl + tt * 512:sl + (tt + 1) * 512], st, sp)
                statmm.append(m)

        ln1_rc = []
        for tt in range(4):
            ln1_rc.append(ln_rows(tt, 4 + tt, tt, 4 + tt,
                                  statmm[-1] if tt == 0 else ln1_rc[-1]))

        p2v = 0
        for kc in range(NKC):
            @blk.vector
            def _(v, kc=kc):
                if kc == 0:
                    v.wait_ge(sems["pe"], ln1_rc[-1])
                for tt in range(4):
                    tp = Df[:, (tt % 2) * 512:(tt % 2) * 512 + 512]
                    xs = A[:, XB(kc) + tt * 512:XB(kc) + (tt + 1) * 512]
                    if tt >= 2:
                        vself(v)
                    v_tt(v, tp, xs, pb[tt][:, :], ALU.mult)
                    vself(v)
                    v_tt(v, xs, tp, pb[4 + tt][:, :], ALU.subtract)
        p2v = cur("vee")
        for tt in range(4):
            bank_read(tt, "vee", p2v)
            bank_read(4 + tt, "vee", p2v)

        if dump:
            @blk.sync
            def _(e):
                for kc in range(NKC):
                    e.wait_ge(sems["vee"], p2v)
                    d_dma(e, "dmx", xn_dram[kc, :, :],
                          A[:, XB(kc):XB(kc) + 2048])

        ev_i = 0

        def evac_store(pe_mark, bank, dst):
            nonlocal ev_i
            sl = ev_i % 8
            ev_i += 1

            @blk.vector
            def _(v, sl=sl, bank=bank, pe_mark=pe_mark):
                v.wait_ge(sems["pe"], pe_mark)
                if evg[sl]:
                    v.wait_ge(sems["dmx"], cur("dmx"))
                    evg[sl].clear()
                v_cp(v, s_ev[:, sl * 512:(sl + 1) * 512], pb[bank][:, :])
            m = cur("vee")
            bank_read(bank, "vee", m)

            @blk.sync
            def _(e, sl=sl, dst=dst, m=m):
                e.wait_ge(sems["vee"], m)
                d_dma(e, "dmx", dst, s_ev[:, sl * 512:(sl + 1) * 512])
            evg[sl]["dmx"] = cur("dmx")
            return cur("dmx")

        def qk_matrix(wsrc, dram, ntt, first_extra_vee=None):
            for eb in range(32):
                i = w_load(wsrc[:, eb * 128:(eb + 1) * 128])
                last = 0
                for tt in range(ntt):
                    bank = tt % 2

                    @blk.tensor
                    def _(t, i=i, eb=eb, tt=tt, bank=bank,
                          fx=(first_extra_vee if eb == 0 and tt == 0 else None)):
                        if fx is not None:
                            t.wait_ge(sems["vee"], fx)
                        if tt == 0:
                            t.wait_ge(sems[f"dmw{i % 2}"], wld[i])
                        bank_take(t, bank)
                        for kc in range(NKC):
                            m = t_mm(t, pb[bank][:, :],
                                     wbf[i % 2][:, kc * 128:(kc + 1) * 128],
                                     A[:, XB(kc) + tt * 512:XB(kc) + (tt + 1) * 512],
                                     kc == 0, kc == NKC - 1)
                        _qk_last[0] = m
                    last = _qk_last[0]
                    evac_store(last, bank, dram[eb, :, tt * 512:(tt + 1) * 512])
                wcons.append(last)

        _qk_last = [0]
        qk_matrix(wqT, q_dram, 2, first_extra_vee=p2v)
        qk_matrix(wkT, k_dram, 4)

        for eb in range(32):
            i = w_load(wvT[:, eb * 128:(eb + 1) * 128])
            stgv = []
            for tt in range(4):
                bank = tt % 2

                @blk.tensor
                def _(t, i=i, tt=tt, bank=bank):
                    if tt == 0:
                        t.wait_ge(sems[f"dmw{i % 2}"], wld[i])
                    bank_take(t, bank)
                    for kc in range(NKC):
                        m = t_mm(t, pb[bank][:, :],
                                 wbf[i % 2][:, kc * 128:(kc + 1) * 128],
                                 A[:, XB(kc) + tt * 512:XB(kc) + (tt + 1) * 512],
                                 kc == 0, kc == NKC - 1)
                    _qk_last[0] = m
                mlast = _qk_last[0]

                @blk.vector
                def _(v, tt=tt, bank=bank, mlast=mlast, eb=eb):
                    v.wait_ge(sems["pe"], mlast)
                    if eb > 0:
                        v.wait_ge(sems["pe"], _vtr_last[0])
                    v_cp(v, Df[:, tt * 512:(tt + 1) * 512], pb[bank][:, :])
                stgv.append(cur("vee"))
                bank_read(bank, "vee", stgv[-1])
            wcons.append(mlast)
            for ktg in range(4):
                tbank = 2 + ktg % 2

                @blk.tensor
                def _(t, ktg=ktg, tbank=tbank):
                    t.wait_ge(sems["vee"], stgv[ktg])
                    bank_take(t, tbank)
                    for j in range(4):
                        kt = ktg * 4 + j
                        m = t_tr(t, pb[tbank][:, j * 128:(j + 1) * 128],
                                 Df[:, kt * 128:(kt + 1) * 128], s_identf[:])
                    _qk_last[0] = m
                _vtr_last[0] = _qk_last[0]
                evac_store(_qk_last[0], tbank,
                           v_dram[eb, :, ktg * 512:(ktg + 1) * 512])
        p3_pe = cur("pe")

        qkvld = {}
        headlast = {}
        expm = {}
        avm = {}
        ctxmul = {}
        drm = {}
        bcm = {}
        ctxdma = {}

        def load_head(e, h):
            b = h % 2
            g = {"pe": headlast[h - 2]} if h - 2 in headlast else {"pe": p3_pe}
            waits(e, g)
            d_dma(e, f"dmq{b}", A[:, QH(b):QH(b) + 1024], q_dram[h, :, :])
            d_dma(e, f"dmq{b}", A[:, KH(b):KH(b) + 2048], k_dram[h, :, :])
            qkvld[h] = d_dma(e, f"dmq{b}", A[:, VH(b):VH(b) + 2048],
                             v_dram[h, :, :])

        p3_dmx = cur("dmx")

        @blk.sync
        def _(e):
            e.wait_ge(sems["dmx"], p3_dmx)
            load_head(e, 0)
            load_head(e, 1)

        for h in range(32):
            hb = h % 2
            for qt in range(2):
                i = h * 2 + qt
                ib = i % 2

                for kt in range(16):
                    bank = kt % 2

                    @blk.tensor
                    def _(t, h=h, qt=qt, i=i, hb=hb, kt=kt, bank=bank):
                        if qt == 0 and kt == 0:
                            t.wait_ge(sems[f"dmq{h % 2}"], qkvld[h])
                        bank_take(t, bank)
                        _scm[(i, kt)] = t_mm(
                            t, pb[bank][:, :],
                            A[:, KH(hb) + kt * 128:KH(hb) + (kt + 1) * 128],
                            A[:, QH(hb) + qt * 512:QH(hb) + (qt + 1) * 512],
                            True, True)

                    @blk.scalar
                    def _(e, i=i, ib=ib, kt=kt, bank=bank):
                        e.wait_ge(sems["pe"], _scm[(i, kt)])
                        if (i - 2, kt) in avm:
                            e.wait_ge(sems["pe"], avm[(i - 2, kt)])
                        elif i < 2:
                            e.wait_ge(sems["pe"], p3_pe)
                        m = a_act(e, A[:, PT(ib) + kt * 512:PT(ib) + (kt + 1) * 512],
                                  pb[bank][:, :], AF.Exp, 0.0, ISQD)
                        expm[(i, kt)] = m
                        bank_read(bank, "sce", m)

                cb = 2 + ib
                db = 4 + ib

                @blk.tensor
                def _(t, i=i, ib=ib, hb=hb, cb=cb, db=db):
                    bank_take(t, cb)
                    bank_take(t, db)
                    for kt in range(16):
                        t.wait_ge(sems["sce"], expm[(i, kt)])
                        pt = A[:, PT(ib) + kt * 512:PT(ib) + (kt + 1) * 512]
                        m = t_mm(t, pb[cb][:, :],
                                 A[:, VH(hb) + kt * 128:VH(hb) + (kt + 1) * 128],
                                 pt, kt == 0, kt == 15)
                        avm[(i, kt)] = m
                        t_mm(t, pb[db][0:1, :], s_onesc[:], pt, kt == 0, kt == 15)
                    _scm[("den", i)] = cur("pe")

                @blk.vector
                def _(v, i=i, db=db):
                    v.wait_ge(sems["pe"], _scm[("den", i)])
                    if i - 1 in bcm:
                        v.wait_ge(sems["pe"], bcm[i - 1])
                    v_cp(v, s_rc[0:1, 0:512], pb[db][0:1, :])
                    vself(v)
                    v.reciprocal(out=s_rc[0:1, 512:1024],
                                 in_=s_rc[0:1, 0:512]).then_inc(
                        sems["vee"], 1)
                    drm[i] = bump("vee")
                    bank_read(db, "vee", drm[i])

                Db = 6 + ib

                @blk.tensor
                def _(t, i=i, Db=Db):
                    t.wait_ge(sems["vee"], drm[i])
                    bank_take(t, Db)
                    bcm[i] = t_mm(t, pb[Db][:, :], s_onesr[0:1, :],
                                  s_rc[0:1, 512:1024], True, True)

                @blk.vector
                def _(v, i=i, h=h, qt=qt, hb=hb, cb=cb, Db=Db):
                    v.wait_ge(sems["pe"], bcm[i])
                    if qt == 0 and h - 2 in ctxdma:
                        v.wait_ge(sems["dmx"], cur("dmx"))
                    cs = A[:, CST(hb) + qt * 512:CST(hb) + (qt + 1) * 512]
                    v_cp(v, cs, pb[cb][:, :])
                    vself(v)
                    ctxmul[i] = v_tt(v, cs, cs, pb[Db][:, :], ALU.mult)
                    bank_read(cb, "vee", ctxmul[i])
                    bank_read(Db, "vee", ctxmul[i])

            headlast[h] = cur("pe")

            @blk.sync
            def _(e, h=h, hb=hb):
                if h + 2 < 32:
                    load_head(e, h + 2)
                e.wait_ge(sems["vee"], ctxmul[h * 2 + 1])
                ctxdma[h] = d_dma(e, "dmx", ctx_dram[h, :, :],
                                  A[:, CST(hb):CST(hb) + 1024])

        p4_pe = cur("pe")
        p4_vee = cur("vee")

        if dump:
            @blk.sync
            def _(e):
                e.wait_ge(sems["pe"], p4_pe)
                for kc in range(16, NKC):
                    d_dma(e, "dmx", xn2_dram[kc - 16, :, :],
                          A[:, XB(kc):XB(kc) + 2048])

        @blk.sync
        def _(e):
            e.wait_ge(sems["pe"], p4_pe)
            e.wait_ge(sems["vee"], p4_vee)
            e.wait_ge(sems["dmx"], ctxdma[31])
            d_dma(e, "dmx", A[:, 0:32768].rearrange("p (h t) -> p h t", t=1024),
                  ctx_dram.rearrange("h p t -> p h t"))
        ctxrb = cur("dmx")

        atv = {}
        for ob in range(32):
            i = w_load(woT[:, ob * 128:(ob + 1) * 128])
            for qt in range(2):
                bank = (ob * 2 + qt) % 2

                @blk.tensor
                def _(t, i=i, ob=ob, qt=qt, bank=bank):
                    if ob == 0 and qt == 0:
                        t.wait_ge(sems["dmx"], ctxrb)
                    if qt == 0:
                        t.wait_ge(sems[f"dmw{i % 2}"], wld[i])
                    bank_take(t, bank)
                    for kc in range(NKC):
                        m = t_mm(t, pb[bank][:, :],
                                 wbf[i % 2][:, kc * 128:(kc + 1) * 128],
                                 A[:, CTX(kc) + qt * 512:CTX(kc) + (qt + 1) * 512],
                                 kc == 0, kc == NKC - 1)
                    _qk_last[0] = m
                mlast = _qk_last[0]
                if qt == 1:
                    wcons.append(mlast)
                sqs = (ob * 2 + qt) % 4

                @blk.vector
                def _(v, ob=ob, qt=qt, bank=bank, mlast=mlast, sqs=sqs):
                    v.wait_ge(sems["pe"], mlast)
                    waits(v, sqg[sqs])
                    sqg[sqs].clear()
                    dst = A[:, ATB(ob) + qt * 512:ATB(ob) + (qt + 1) * 512]
                    v_cp(v, dst, pb[bank][:, :])
                    vself(v)
                    atv[(ob, qt)] = v_tt(v, s_sq[:, sqs * 1024:sqs * 1024 + 512],
                                         dst, dst, ALU.mult)
                bank_read(bank, "vee", atv[(ob, qt)])

                @blk.tensor
                def _(t, ob=ob, qt=qt, sqs=sqs):
                    t.wait_ge(sems["vee"], atv[(ob, qt)])
                    st, sp = ob == 0, ob == 31
                    dst = A[:, ATB(ob) + qt * 512:ATB(ob) + (qt + 1) * 512]
                    if st:
                        bank_take(t, 2 + qt)
                        bank_take(t, 4 + qt)
                    t_mm(t, pb[2 + qt][0:1, :], s_onesc[:], dst, st, sp)
                    m = t_mm(t, pb[4 + qt][0:1, :], s_onesc[:],
                             s_sq[:, sqs * 1024:sqs * 1024 + 512], st, sp)
                    sqg[sqs]["pe"] = m

        ln2_rc = []
        for qt in range(2):
            ln2_rc.append(ln_rows(2 + qt, 4 + qt, 2 + qt, 4 + qt,
                                  cur("pe") if qt == 0 else ln2_rc[-1]))

        xol, x1wr, p6v = {}, {}, {}
        for kc in range(NKC):
            @blk.sync
            def _(e, kc=kc):
                if kc >= 2:
                    e.wait_ge(sems["vee"], p6v[kc - 2])
                xol[kc] = d_dma(e, "dmx",
                                Df[:, (kc % 2) * 1024:(kc % 2) * 1024 + 1024],
                                xT[kc * 128:(kc + 1) * 128, 0:1024])

            sqs = kc % 4

            @blk.vector
            def _(v, kc=kc, sqs=sqs):
                v.wait_ge(sems["dmx"], cur("dmx"))
                if kc == 0:
                    v.wait_ge(sems["pe"], ln2_rc[-1])
                waits(v, sqg[sqs])
                sqg[sqs].clear()
                for qt in range(2):
                    xt = Df[:, 2048 + (kc % 2) * 1024 + qt * 512:
                            2048 + (kc % 2) * 1024 + (qt + 1) * 512]
                    ats = A[:, ATB(kc) + qt * 512:ATB(kc) + (qt + 1) * 512]
                    v_tt(v, xt, ats, pb[2 + qt][:, :], ALU.mult)
                    vself(v)
                    v_tt(v, xt, xt, pb[4 + qt][:, :], ALU.subtract)
                    vself(v)
                    v_tt(v, xt, xt,
                         Df[:, (kc % 2) * 1024 + qt * 512:
                             (kc % 2) * 1024 + (qt + 1) * 512], ALU.add)
                    vself(v)
                _x1m[0] = cur("vee")
                for qt in range(2):
                    xt = Df[:, 2048 + (kc % 2) * 1024 + qt * 512:
                            2048 + (kc % 2) * 1024 + (qt + 1) * 512]
                    x1s = A[:, X1B(kc) + qt * 512:X1B(kc) + (qt + 1) * 512]
                    v_cp(v, x1s, xt)
                    vself(v)
                    v_tt(v, s_sq[:, sqs * 1024 + qt * 512:
                                 sqs * 1024 + (qt + 1) * 512],
                         x1s, x1s, ALU.mult)
                p6v[kc] = cur("vee")
            m_x1 = _x1m[0]

            @blk.sync
            def _(e, kc=kc, m_x1=m_x1):
                e.wait_ge(sems["vee"], m_x1)
                x1wr[kc] = d_dma(e, "dmx", x1_dram[kc, :, :],
                                 Df[:, 2048 + (kc % 2) * 1024:
                                     2048 + (kc % 2) * 1024 + 1024])

            @blk.tensor
            def _(t, kc=kc, sqs=sqs):
                t.wait_ge(sems["vee"], p6v[kc])
                st, sp = kc == 0, kc == 31
                if st:
                    for qt in range(2):
                        bank_take(t, 0 + qt)
                        bank_take(t, 6 + qt)
                for qt in range(2):
                    x1s = A[:, X1B(kc) + qt * 512:X1B(kc) + (qt + 1) * 512]
                    t_mm(t, pb[0 + qt][0:1, :], s_onesc[:], x1s, st, sp)
                    m = t_mm(t, pb[6 + qt][0:1, :], s_onesc[:],
                             s_sq[:, sqs * 1024 + qt * 512:
                                  sqs * 1024 + (qt + 1) * 512], st, sp)
                    sqg[sqs]["pe"] = m

        ln3_rc = []
        for qt in range(2):
            ln3_rc.append(ln_rows(0 + qt, 6 + qt, 0 + qt, 6 + qt,
                                  cur("pe") if qt == 0 else ln3_rc[-1]))

        p7av = 0
        for kc in range(NKC):
            @blk.vector
            def _(v, kc=kc):
                if kc == 0:
                    v.wait_ge(sems["pe"], ln3_rc[-1])
                    v.wait_ge(sems["dmx"], cur("dmx"))
                for qt in range(2):
                    tp = Df[:, 2048 + qt * 512:2048 + (qt + 1) * 512]
                    x1s = A[:, X1B(kc) + qt * 512:X1B(kc) + (qt + 1) * 512]
                    vself(v)
                    v_tt(v, tp, x1s, pb[0 + qt][:, :], ALU.mult)
                    vself(v)
                    v_tt(v, x1s, tp, pb[6 + qt][:, :], ALU.subtract)
        p7av = cur("vee")
        for qt in range(2):
            bank_read(0 + qt, "vee", p7av)
            bank_read(6 + qt, "vee", p7av)

        @blk.gpsimd
        def _(e):
            e.wait_ge(sems["vee"], p6v[7])
            d_dma(e, "dmc", A[:, AT0:AT0 + 4096].rearrange(
                "p (kc e) -> p kc e", e=128),
                ATd.rearrange("(kc p) e -> p kc e", p=128))
            d_dma(e, "dmc", A[:, BT0:BT0 + 4096], BTd[:, :])
        abld = cur("dmc")

        lgmm, wexpm, htm, hwv = {}, {}, {}, {}
        for qt in range(2):
            @blk.tensor
            def _(t, qt=qt):
                if qt == 0:
                    t.wait_ge(sems["vee"], p7av)
                    t.wait_ge(sems["dmc"], init_dmw)
                bank_take(t, 2 + qt)
                for kc in range(NKC):
                    m = t_mm(t, pb[2 + qt][0:8, :],
                             s_rt[:, kc * 8:(kc + 1) * 8],
                             A[:, X1B(kc) + qt * 512:X1B(kc) + (qt + 1) * 512],
                             kc == 0, kc == NKC - 1)
                lgmm[qt] = m

            @blk.vector
            def _(v, qt=qt):
                v.wait_ge(sems["pe"], lgmm[qt])
                if qt == 1:
                    v.wait_ge(sems["pe"], _lgtr[3])
                v_cp(v, s_lg[0:8, :], pb[2 + qt][0:8, :])
            lgev = cur("vee")
            bank_read(2 + qt, "vee", lgev)

            @blk.tensor
            def _(t, qt=qt, lgev=lgev):
                t.wait_ge(sems["vee"], lgev)
                if qt == 0:
                    t.wait_ge(sems["dmx"], init_dmx)
                    bank_take(t, 4)
                for c in range(4):
                    _lgtr[c] = t_tr(t, pb[4][:, (qt * 4 + c) * 8:(qt * 4 + c + 1) * 8],
                                    s_lg[0:8, c * 128:(c + 1) * 128],
                                    s_identf[0:8, 0:8])

            @blk.vector
            def _(v, qt=qt):
                for c in range(4):
                    cc = qt * 4 + c
                    v.wait_ge(sems["pe"], _lgtr[c])
                    lgs = pb[4][:, cc * 8:(cc + 1) * 8]
                    v.reduce_max(out=s_m1[:, cc:cc + 1], in_=lgs,
                                 axis=AX.X).then_inc(sems["vee"], 1)
                    bump("vee")
                    vself(v)
                    v.tensor_single_scalar(
                        out=s_msk1[:, cc * 8:(cc + 1) * 8], in_=lgs,
                        scalar=s_m1[:, cc:cc + 1], op=ALU.is_ge).then_inc(
                        sems["vee"], 1)
                    bump("vee")
                    vself(v)
                    v.scalar_tensor_tensor(
                        out=s_lg2[:, cc * 8:(cc + 1) * 8],
                        in0=s_msk1[:, cc * 8:(cc + 1) * 8], scalar=-1e30,
                        in1=lgs, op0=ALU.mult, op1=ALU.add).then_inc(
                        sems["vee"], 1)
                    bump("vee")
                    vself(v)
                    v.reduce_max(out=s_m2[:, cc:cc + 1],
                                 in_=s_lg2[:, cc * 8:(cc + 1) * 8],
                                 axis=AX.X).then_inc(sems["vee"], 1)
                    bump("vee")
                    vself(v)
                    v.tensor_single_scalar(
                        out=s_msk2[:, cc * 8:(cc + 1) * 8],
                        in_=s_lg2[:, cc * 8:(cc + 1) * 8],
                        scalar=s_m2[:, cc:cc + 1], op=ALU.is_ge).then_inc(
                        sems["vee"], 1)
                    bump("vee")
            _tk[qt] = cur("vee")

        @blk.vector
        def _(v):
            v.wait_ge(sems["vee"], _tk[1])
            vself(v)
            v_tt(v, s_den[:, :], s_m2[:, :], s_m1[:, :], ALU.subtract)
        tkd = cur("vee")

        @blk.scalar
        def _(e):
            e.wait_ge(sems["vee"], tkd)
            a_act(e, s_e2[:, :], s_den[:, :], AF.Exp, 0.0, 1.0)
        tke = cur("sce")

        @blk.vector
        def _(v):
            v.wait_ge(sems["sce"], tke)
            v.tensor_scalar(out=s_den[:, :], in0=s_e2[:, :], scalar1=1.0,
                            scalar2=0.0, op0=ALU.add,
                            op1=ALU.add).then_inc(sems["vee"], 1)
            bump("vee")
            vself(v)
            v.reciprocal(out=s_w2[:, :], in_=s_den[:, :]).then_inc(sems["vee"], 1)
            bump("vee")
            vself(v)
            v_tt(v, s_w2[:, :], s_w2[:, :], s_e2[:, :], ALU.mult)
            vself(v)
            v.tensor_scalar(out=s_w1[:, :], in0=s_w2[:, :], scalar1=-1.0,
                            scalar2=1.0, op0=ALU.mult, op1=ALU.add).then_inc(
                sems["vee"], 1)
            bump("vee")
            for cc in range(8):
                vself(v)
                v.tensor_single_scalar(
                    out=s_wf[:, cc * 8:(cc + 1) * 8],
                    in_=s_msk1[:, cc * 8:(cc + 1) * 8],
                    scalar=s_w1[:, cc:cc + 1], op=ALU.mult).then_inc(
                    sems["vee"], 1)
                bump("vee")
                v.tensor_single_scalar(
                    out=s_msk2[:, cc * 8:(cc + 1) * 8],
                    in_=s_msk2[:, cc * 8:(cc + 1) * 8],
                    scalar=s_w2[:, cc:cc + 1], op=ALU.mult).then_inc(
                    sems["vee"], 1)
                bump("vee")
                vself(v)
                v_tt(v, s_wf[:, cc * 8:(cc + 1) * 8],
                     s_wf[:, cc * 8:(cc + 1) * 8],
                     s_msk2[:, cc * 8:(cc + 1) * 8], ALU.add)

        tkw = cur("vee")
        bank_read(4, "vee", tkw)

        @blk.tensor
        def _(t):
            t.wait_ge(sems["vee"], tkw)
            bank_take(t, 5)
            bank_take(t, 6)
            for cc in range(8):
                bk = 5 + cc // 4
                _lgtr[cc % 4] = t_tr(t, pb[bk][0:8, (cc % 4) * 128:
                                               (cc % 4 + 1) * 128],
                                     s_wf[:, cc * 8:(cc + 1) * 8], s_identf[:])
        wtr = cur("pe")

        @blk.vector
        def _(v):
            v.wait_ge(sems["pe"], wtr)
            v_cp(v, s_wT[0:8, 0:512], pb[5][0:8, :])
            v_cp(v, s_wT[0:8, 512:1024], pb[6][0:8, :])
        wtev = cur("vee")
        bank_read(5, "vee", wtev)
        bank_read(6, "vee", wtev)

        for qt in range(2):
            @blk.tensor
            def _(t, qt=qt):
                if qt == 0:
                    t.wait_ge(sems["vee"], wtev)
                bank_take(t, 0 + qt)
                wexpm[qt] = t_mm(t, pb[0 + qt][:, :], s_rep[0:8, :],
                                 s_wT[0:8, qt * 512:(qt + 1) * 512], True, True)
                if qt == 0:
                    t.wait_ge(sems["dmc"], abld)
                bank_take(t, 6 + qt)
                for kc in range(NKC):
                    m = t_mm(t, pb[6 + qt][:, :],
                             A[:, AT0 + kc * 128:AT0 + (kc + 1) * 128],
                             A[:, X1B(kc) + qt * 512:X1B(kc) + (qt + 1) * 512],
                             kc == 0, kc == NKC - 1)
                htm[qt] = m

            @blk.vector
            def _(v, qt=qt):
                v.wait_ge(sems["pe"], htm[qt])
                hs = s_hw[:, qt * 512:(qt + 1) * 512]
                v_cp(v, hs, pb[6 + qt][:, :])
                vself(v)
                hwv[qt] = v_tt(v, hs, hs, pb[0 + qt][:, :], ALU.mult)
            bank_read(0 + qt, "vee", hwv[qt])
            bank_read(6 + qt, "vee", hwv[qt])

        ffv = {}
        for ob in range(32):
            i = w_load(wbT[:, ob * 128:(ob + 1) * 128])
            for qt in range(2):
                bank = 2 + (ob * 2 + qt) % 2

                @blk.tensor
                def _(t, i=i, ob=ob, qt=qt, bank=bank):
                    if ob == 0 and qt == 0:
                        t.wait_ge(sems["vee"], hwv[1])
                    if qt == 0:
                        t.wait_ge(sems[f"dmw{i % 2}"], wld[i])
                    bank_take(t, bank)
                    for kc in range(NKC):
                        t_mm(t, pb[bank][:, :],
                             wbf[i % 2][:, kc * 128:(kc + 1) * 128],
                             A[:, X1B(kc) + qt * 512:X1B(kc) + (qt + 1) * 512],
                             kc == 0, False)
                    m = t_mm(t, pb[bank][:, :],
                             A[:, BT0 + ob * 128:BT0 + (ob + 1) * 128],
                             s_hw[:, qt * 512:(qt + 1) * 512], False, True)
                    _qk_last[0] = m
                mlast = _qk_last[0]
                if qt == 1:
                    wcons.append(mlast)
                sl = (ob * 2 + qt) % 8
                sqs = (ob * 2 + qt) % 4

                @blk.vector
                def _(v, ob=ob, mlast=mlast, bank=bank, sl=sl, sqs=sqs):
                    v.wait_ge(sems["pe"], mlast)
                    if "dmx" in evg[sl]:
                        v.wait_ge(sems["dmx"], cur("dmx"))
                        del evg[sl]["dmx"]
                    waits(v, evg[sl])
                    evg[sl].clear()
                    waits(v, sqg[sqs])
                    sqg[sqs].clear()
                    v.tensor_single_scalar(
                        out=s_ev[:, sl * 512:(sl + 1) * 512],
                        in_=pb[bank][:, :], scalar=s_bb[:, ob:ob + 1],
                        op=ALU.add).then_inc(sems["vee"], 1)
                    bump("vee")
                    vself(v)
                    v_tt(v, s_sq[:, sqs * 1024:sqs * 1024 + 512],
                         s_ev[:, sl * 512:(sl + 1) * 512],
                         s_ev[:, sl * 512:(sl + 1) * 512], ALU.mult)
                ffv[(ob, qt)] = cur("vee")
                bank_read(bank, "vee", ffv[(ob, qt)])

                @blk.sync
                def _(e, ob=ob, qt=qt, sl=sl):
                    e.wait_ge(sems["vee"], ffv[(ob, qt)])
                    d_dma(e, "dmx", ffn_dram[ob, :, qt * 512:(qt + 1) * 512],
                          s_ev[:, sl * 512:(sl + 1) * 512])
                evg[sl]["dmx"] = cur("dmx")

                @blk.tensor
                def _(t, ob=ob, qt=qt, sl=sl, sqs=sqs):
                    t.wait_ge(sems["vee"], ffv[(ob, qt)])
                    st, sp = ob == 0, ob == 31
                    if st:
                        bank_take(t, 4 + qt)
                        bank_take(t, 0 + qt)
                    t_mm(t, pb[4 + qt][0:1, :], s_onesc[:],
                         s_ev[:, sl * 512:(sl + 1) * 512], st, sp)
                    m = t_mm(t, pb[0 + qt][0:1, :], s_onesc[:],
                             s_sq[:, sqs * 1024:sqs * 1024 + 512], st, sp)
                    evg[sl]["pe"] = m
                    sqg[sqs]["pe"] = m

        ln4_rc = []
        for qt in range(2):
            ln4_rc.append(ln_rows(4 + qt, 0 + qt, 4 + qt, 0 + qt,
                                  cur("pe") if qt == 0 else ln4_rc[-1]))

        p8_pre = cur("dmx")
        p8ld, p8v, ydma = {}, {}, {}
        for kc in range(NKC):
            @blk.sync
            def _(e, kc=kc):
                if kc == 0:
                    e.wait_ge(sems["dmx"], p8_pre)
                if kc >= 2:
                    e.wait_ge(sems["vee"], p8v[kc - 2])
                d_dma(e, "dmx", Df[:, (kc % 2) * 1024:(kc % 2) * 1024 + 1024],
                      x1_dram[kc, :, :])
                p8ld[kc] = d_dma(e, "dmx",
                                 s_qkstg[:, (kc % 2) * 1024:(kc % 2) * 1024 + 1024],
                                 ffn_dram[kc, :, :])

            @blk.vector
            def _(v, kc=kc):
                v.wait_ge(sems["dmx"], cur("dmx"))
                if kc == 0:
                    v.wait_ge(sems["pe"], ln4_rc[-1])
                if kc - 2 in ydma:
                    v.wait_ge(sems[f"dmo{kc % 2}"], ydma[kc - 2])
                for qt in range(2):
                    ys = Df[:, 2048 + (kc % 2) * 1024 + qt * 512:
                            2048 + (kc % 2) * 1024 + (qt + 1) * 512]
                    fs = s_qkstg[:, (kc % 2) * 1024 + qt * 512:
                                 (kc % 2) * 1024 + (qt + 1) * 512]
                    v_tt(v, ys, fs, pb[4 + qt][:, :], ALU.mult)
                    vself(v)
                    v_tt(v, ys, ys, pb[0 + qt][:, :], ALU.subtract)
                    vself(v)
                    v_tt(v, ys, ys,
                         Df[:, (kc % 2) * 1024 + qt * 512:
                             (kc % 2) * 1024 + (qt + 1) * 512], ALU.add)
                p8v[kc] = cur("vee")

            @blk.sync
            def _(e, kc=kc):
                e.wait_ge(sems["vee"], p8v[kc])
                ydma[kc] = d_dma(e, f"dmo{kc % 2}", yT[kc * 128:(kc + 1) * 128, :],
                                 Df[:, 2048 + (kc % 2) * 1024:
                                     2048 + (kc % 2) * 1024 + 1024])

        @blk.sync
        def _(e):
            e.wait_ge(sems["dmo0"], ydma[30])
            e.wait_ge(sems["dmo1"], ydma[31])

    return nc


_scm = {}
_lgtr = {}
_tk = {}
_vtr_last = [0]
_x1m = [0]
_x1m_frozen = {}
_p6pe = {}

_NC = None


def _get_nc():
    global _NC
    if _NC is None:
        _scm.clear()
        _lgtr.clear()
        _tk.clear()
        _NC = build_program()
    return _NC


def kernel(**inputs):
    x = np.asarray(inputs["L0_attn_x"], np.float32)
    w_q = np.asarray(inputs["L0_attn_w_q"], np.float32)
    w_k = np.asarray(inputs["L0_attn_w_k"], np.float32)
    w_v = np.asarray(inputs["L0_attn_w_v"], np.float32)
    w_o = np.asarray(inputs["L0_attn_w_o"], np.float32)
    W_base = np.asarray(inputs["L0_ffn_W_base"], np.float32)
    b_base = np.asarray(inputs["L0_ffn_b_base"], np.float32)
    W_router = np.asarray(inputs["L0_ffn_W_router"], np.float32)
    Am = np.asarray(inputs["L0_ffn_A"], np.float32)
    Bm = np.asarray(inputs["L0_ffn_B"], np.float32)

    shared = {
        "wqT": np.ascontiguousarray(w_q.T),
        "wkT": np.ascontiguousarray(w_k.T),
        "wvT": np.ascontiguousarray(w_v.T),
        "woT": np.ascontiguousarray(w_o.T),
        "wbT": np.ascontiguousarray(W_base.T),
        "bbC": np.ascontiguousarray(b_base.reshape(32, 128).T),
        "wrT": np.ascontiguousarray(W_router.T),
        "ATd": np.ascontiguousarray(Am.transpose(2, 0, 1).reshape(D, 128)),
        "BTd": np.ascontiguousarray(Bm.transpose(0, 2, 1).reshape(128, D)),
        "identI": np.eye(128, dtype=ml_dtypes.bfloat16),
        "identF": np.eye(128, dtype=np.float32),
        "REPI": (2.0 * (np.arange(128)[None, :] // 16 ==
                        np.arange(8)[:, None])).astype(ml_dtypes.bfloat16),
    }
    in_maps = []
    for c in range(8):
        b, half = c // 2, c % 2
        own = x[b, half * 1024:(half + 1) * 1024].T
        oth = x[b, (1 - half) * 1024:(2 - half) * 1024].T
        m = dict(shared)
        m["xT"] = np.ascontiguousarray(np.concatenate([own, oth], axis=1))
        in_maps.append(m)

    nc = _get_nc()
    res = run_bass_kernel_spmd(nc, in_maps, core_ids=list(range(8)))
    out = np.empty((4, 2048, D), np.float32)
    for c in range(8):
        b, half = c // 2, c % 2
        out[b, half * 1024:(half + 1) * 1024, :] = res.results[c]["yT"].T
    return out


if __name__ == "__main__":
    nc = _get_nc()
    print("build ok")
